# version 05
# baseline (speedup 1.0000x reference)
import numpy as np
import concourse.bass as bass
import concourse.mybir as mybir
from concourse.bass_utils import run_bass_kernel_spmd

F32 = mybir.dt.float32
BF16 = mybir.dt.bfloat16
I32 = mybir.dt.int32
AF = mybir.ActivationFunctionType
ALU = mybir.AluOpType
AX = mybir.AxisListType.X

T = 2048
D = 1024
NT = T // 128
ND = D // 128

PE, ACT, DVE, POOL, SP = "pe", "act", "dve", "pool", "sp"
_ENGS = (PE, ACT, DVE, POOL, SP)


class Op:
    __slots__ = ("eng", "fn", "deps", "idx", "is_dma", "is_async", "sig", "val")

    def __init__(self, eng, fn, deps, idx, is_dma, is_async=False):
        self.eng = eng
        self.fn = fn
        self.deps = list(deps)
        self.idx = idx
        self.is_dma = is_dma
        self.is_async = is_async
        self.sig = False
        self.val = 0


class Prog:

    def __init__(self, nc):
        self.nc = nc
        self.ops = []
        self.barrier = []

    def set_barrier(self):
        seen = {}
        for o in self.ops:
            seen[o.eng] = o
        self.barrier = list(seen.values())

    def op(self, eng, fn, deps=(), async_=True):
        deps = _flat(deps) + self.barrier
        o = Op(eng, fn, deps, len(self.ops), False, async_)
        self.ops.append(o)
        if async_:
            o.sig = True
        return o

    def dma(self, eng, out, in_, deps=()):
        deps = _flat(deps) + self.barrier

        def fn(e, out=out, in_=in_):
            return e.dma_start(out=out, in_=in_)

        o = Op(eng, fn, deps, len(self.ops), True)
        self.ops.append(o)
        return o

    def emit(self, sems):
        for o in self.ops:
            if o.is_dma or o.is_async:
                o.sig = True
            for d in o.deps:
                assert d.idx < o.idx, "deps must point backwards"
                if d.eng != o.eng or d.is_dma or d.is_async:
                    d.sig = True
        cnt = {e: 0 for e in _ENGS}
        for o in self.ops:
            if o.sig:
                cnt[o.eng] += 16 if o.is_dma else 1
                o.val = cnt[o.eng]
        nc = self.nc
        blocks = {e: [o for o in self.ops if o.eng == e] for e in _ENGS}

        def emit_engine(eng_name, engobj):
            waited = {e: 0 for e in _ENGS}
            for o in blocks[eng_name]:
                for d in o.deps:
                    if d.eng == eng_name and not d.is_dma and not d.is_async:
                        continue
                    if d.val > waited[d.eng]:
                        engobj.wait_ge(sems[d.eng], d.val)
                        waited[d.eng] = d.val
                inst = o.fn(engobj)
                if o.sig:
                    inst.then_inc(sems[o.eng], 16 if o.is_dma else 1)

        with nc.Block() as block:
            @block.tensor
            def _(e):
                emit_engine(PE, e)

            @block.scalar
            def _(e):
                emit_engine(ACT, e)

            @block.vector
            def _(e):
                emit_engine(DVE, e)

            @block.gpsimd
            def _(e):
                emit_engine(POOL, e)

            @block.sync
            def _(e):
                emit_engine(SP, e)


def _flat(toks):
    out = []
    for t in (toks if isinstance(toks, (list, tuple)) else [toks]):
        if t is None:
            continue
        if isinstance(t, (list, tuple)):
            out.extend(_flat(t))
        else:
            out.append(t)
    return out


class Banks:

    def __init__(self, nc):
        self.t = [nc.psum_tensor(f"psb{k}", [128, 512], F32) for k in range(8)]
        self.w = [None] * 8
        self.r = [[] for _ in range(8)]

    def f32(self, k):
        return self.t[k]

    def bf16(self, k):
        return self.t[k][:].bitcast(BF16)

    def war(self, k):
        dep = self.r[k] if self.r[k] else ([self.w[k]] if self.w[k] else [])
        return list(dep)

    def wrote(self, k, tok):
        self.w[k] = tok
        self.r[k] = []

    def read(self, k, tok):
        self.r[k].append(tok)


class Sbuf:

    def __init__(self, nc, name, words):
        self.t = nc.sbuf_tensor(name, [128, words], F32)
        self.h = self.t.__enter__()

    def f32(self, off, n):
        return self.h[:, off:off + n]

    def bf16(self, off, nwords):
        return self.h[:, off:off + nwords].bitcast(BF16)

    def i32(self, off, n):
        return self.h[:, off:off + n].bitcast(I32)


def mm_group(p, banks, bank, out_ap, terms, extra=()):
    war = banks.war(bank)
    tok = None
    n = len(terms)
    for i, (l, r, dl) in enumerate(terms):
        deps = _flat([dl, extra]) + (war if i == 0 else [])

        def fn(e, l=l, r=r, s=(i == 0), st=(i == n - 1)):
            return e.matmul(out_ap, l, r, start=s, stop=st, skip_group_check=True)

        tok = p.op(PE, fn, deps=deps)
    banks.wrote(bank, tok)
    return tok


def transpose_op(p, banks, bank, out_ap, in_ap, ident, deps):
    war = banks.war(bank)
    if type(ident).__name__ != "AP":
        ident = ident[:]

    def fn(e, out_ap=out_ap, in_ap=in_ap, ident=ident):
        return e.transpose(out_ap, in_ap, ident)

    tok = p.op(PE, fn, deps=_flat(deps) + war)
    banks.wrote(bank, tok)
    return tok


def pool_fence(p, toks):
    return p.op(POOL, lambda e: e.nop(), deps=_flat([toks]))


def bank_read(p, banks, bank, eng, fn, deps=(), async_=True):
    tok = p.op(eng, fn, deps=_flat(deps) + ([banks.w[bank]] if banks.w[bank] else []),
               async_=async_)
    banks.read(bank, tok)
    return tok


def emit_consts(p, ident_bf, ident_f32, trimask, ones32, onesbf):
    o1 = p.op(DVE, lambda e: e.memset(ident_bf[:], 0.0))
    o2 = p.op(DVE, lambda e: e.memset(ident_f32[:], 0.0))
    o3 = p.op(DVE, lambda e: e.memset(trimask[:], 0.0))
    o4 = p.op(DVE, lambda e: e.memset(ones32[:], 1.0))
    o5 = p.op(DVE, lambda e: e.memset(onesbf[:], 1.0))
    t1 = p.op(POOL, lambda e: e.affine_select(
        out=ident_bf[:], in_=ident_bf[:], compare_op=ALU.not_equal,
        fill=1.0, base=0, pattern=[[-1, 128]], channel_multiplier=1), deps=[o1])
    t2 = p.op(POOL, lambda e: e.affine_select(
        out=ident_f32[:], in_=ident_f32[:], compare_op=ALU.not_equal,
        fill=1.0, base=0, pattern=[[-1, 128]], channel_multiplier=1), deps=[o2])
    t3 = p.op(POOL, lambda e: e.affine_select(
        out=trimask[:], in_=trimask[:], compare_op=ALU.is_ge,
        fill=-1.0e30, base=0, pattern=[[-1, 128]], channel_multiplier=1), deps=[o3])
    return t1, t2, t3, o4, o5


LN_SKIP = set()


def emit_ln(p, X, XN, ST, c0, scratch32, dep_x, eps=1e-5):
    SUM = ST[:, c0 + 0 * NT:c0 + 1 * NT]
    SQ = ST[:, c0 + 1 * NT:c0 + 2 * NT]
    NM = ST[:, c0 + 2 * NT:c0 + 3 * NT]
    RSTD = ST[:, c0 + 3 * NT:c0 + 4 * NT]
    sq_toks, sum_toks = [], []
    sq_prev = None
    for i in range(NT):
        Xi = X[:, i * D:(i + 1) * D]

        def fsq0(e, Xi=Xi):
            return e.activation(scratch32[:, 0:D], Xi, AF.Square)

        def fsq1(e, i=i):
            return e.tensor_reduce(SQ[:, i:i + 1], scratch32[:, 0:D], axis=AX, op=ALU.add)

        def fsum(e, Xi=Xi, i=i):
            return e.tensor_reduce(SUM[:, i:i + 1], Xi, axis=AX, op=ALU.add)

        tq0 = p.op(ACT, fsq0, deps=[dep_x, sq_prev])
        tq1 = p.op(DVE, fsq1, deps=[tq0], async_=True)
        sq_prev = tq1
        sq_toks.append(tq1)
        if "sum" not in LN_SKIP:
            sum_toks.append(p.op(DVE, fsum, deps=[dep_x], async_=True))
    if "tail" in LN_SKIP:
        return sq_toks
    t1 = p.op(DVE, lambda e: e.tensor_scalar(NM[:], SUM[:], -1.0 / D, None, op0=ALU.mult),
              deps=sum_toks)
    t2 = p.op(DVE, lambda e: e.tensor_scalar(SQ[:], SQ[:], 1.0 / D, None, op0=ALU.mult),
              deps=sq_toks)
    t3 = p.op(DVE, lambda e: e.tensor_tensor(SUM[:], NM[:], NM[:], op=ALU.mult),
              deps=[t1, t2])
    t4 = p.op(DVE, lambda e: e.tensor_tensor(SQ[:], SQ[:], SUM[:], op=ALU.subtract),
              deps=[t3])
    EPSC = ST[:, SC_EPS:SC_EPS + 1]
    tme = p.op(DVE, lambda e: e.memset(EPSC, eps))
    t5 = p.op(ACT, lambda e: e.activation(SUM[:], SQ[:], AF.Ln, bias=EPSC), deps=[t4, tme])
    t6 = p.op(ACT, lambda e: e.activation(RSTD[:], SUM[:], AF.Exp, scale=-0.5), deps=[t5])
    out_toks = []
    if "norm" in LN_SKIP:
        return [t6]
    for i in range(NT):
        def fn(e, i=i):
            return e.tensor_scalar(XN[:, i * D:(i + 1) * D], X[:, i * D:(i + 1) * D],
                                   NM[:, i:i + 1], RSTD[:, i:i + 1],
                                   op0=ALU.add, op1=ALU.mult)

        out_toks.append(p.op(DVE, fn, deps=[t6, dep_x]))
    return out_toks


def emit_cast_transpose(p, banks, src_tiles, dst_bf16, stg, ident_bf, deps_tiles,
                        war_dst=(), cast_eng=DVE, width=D, dst_stride=None,
                        produce=None):
    nj = width // 128
    assert nj <= 8
    if dst_stride is None:
        dst_stride = T
    ntile = len(src_tiles)
    evac_toks = []
    cast_toks = [None, None]
    tr_toks = [[] for _ in range(nj)]
    last_tr = [None, None]
    out_toks = []
    for i in range(ntile):
        b = i % 2

        pdeps = _flat([deps_tiles[i]]) + ([last_tr[b]] if last_tr[b] else [])
        if produce is not None:
            cast_toks[b] = produce(i, stg[b], pdeps)
        else:
            def fcast(e, i=i, b=b):
                return e.tensor_copy(stg[b], src_tiles[i])

            cast_toks[b] = p.op(cast_eng, fcast, deps=pdeps)
        trs = []
        for j in range(nj):
            slot = i % 8
            tok = transpose_op(
                p, banks, j,
                banks.bf16(j)[:, slot * 128:(slot + 1) * 128],
                stg[b][:, j * 128:(j + 1) * 128],
                ident_bf, deps=[cast_toks[b]])
            trs.append(tok)
            tr_toks[j].append(tok)
        last_tr[b] = trs[-1]
        if i % 8 == 7 or i == ntile - 1:
            g = i // 8
            nsl = (i % 8) + 1
            for j in range(nj):
                def fev(e, j=j, g=g, nsl=nsl):
                    return e.tensor_copy(
                        dst_bf16[:, j * dst_stride + g * 1024:j * dst_stride + g * 1024 + nsl * 128],
                        banks.bf16(j)[:, 0:nsl * 128])

                tok = bank_read(p, banks, j, DVE, fev, deps=_flat([tr_toks[j], war_dst]))
                evac_toks.append(tok)
                tr_toks[j] = []
    return evac_toks


def bcast_col(p, banks, bank, ones32, src11, dst_col, deps):
    war = banks.war(bank)

    def fmm(e):
        return e.matmul(banks.f32(bank)[:, 0:1], ones32[0:1, 0:128], src11,
                        start=True, stop=True, skip_group_check=True)

    t1 = p.op(PE, fmm, deps=_flat(deps) + war)
    banks.wrote(bank, t1)

    def fev(e):
        return e.tensor_copy(dst_col, banks.f32(bank)[:, 0:1])

    return bank_read(p, banks, bank, DVE, fev)



class _Alloc:
    def __init__(self, nc):
        self.nc = nc
        self._ctx = []

    def sbuf(self, name, shape, dt):
        cm = self.nc.sbuf_tensor(name, shape, dt)
        h = cm.__enter__()
        self._ctx.append(cm)
        return h

    def psum(self, name, shape, dt):
        cm = self.nc.psum_tensor(name, shape, dt)
        h = cm.__enter__()
        self._ctx.append(cm)
        return h

    def sem(self, name):
        cm = self.nc.semaphore(name)
        h = cm.__enter__()
        self._ctx.append(cm)
        return h


def make_banks(al):
    b = Banks.__new__(Banks)
    b.t = [al.psum(f"psb{k}", [128, 512], F32) for k in range(8)]
    b.w = [None] * 8
    b.r = [[] for _ in range(8)]
    return b


def std_env(nc, arena_words):
    al = _Alloc(nc)
    env = {}
    env["al"] = al
    env["AR"] = al.sbuf("arena", [128, arena_words], F32)
    env["ST"] = al.sbuf("stats", [128, 320], F32)
    env["SCR"] = al.sbuf("scr32", [128, 2048], F32)
    env["ONES32"] = al.sbuf("ones32", [128, 128], F32)
    env["ONESBF"] = al.sbuf("onesbf", [128, 128], BF16)
    env["IDB"] = al.sbuf("identb", [128, 128], BF16)
    env["IDF"] = al.sbuf("identf", [128, 128], F32)
    env["TRI"] = al.sbuf("trim", [128, 128], F32)
    env["banks"] = make_banks(al)
    env["sems"] = {e: al.sem("sem_" + e) for e in _ENGS}
    return env


def arena_views(AR):
    def f32(off, n):
        return AR[:, off:off + n]

    def bf16(off, nwords):
        return AR[:, off:off + nwords].bitcast(BF16)

    def i32(off, n):
        return AR[:, off:off + n].bitcast(I32)

    return f32, bf16, i32


def emit_quant_rounding(p, src_ap, scale_col, inv_col, out_bf_ap, scr, deps):
    NI, NF, DD, C1, QQ = scr["NI"], scr["NF"], scr["DD"], scr["C1"], scr["QQ"]
    t1 = p.op(DVE, lambda e: e.tensor_scalar(NI, src_ap, scale_col, None, op0=ALU.mult),
              deps=deps)
    t2 = p.op(DVE, lambda e: e.tensor_copy(NF, NI), deps=[t1])
    t3 = p.op(DVE, lambda e: e.scalar_tensor_tensor(
        DD, src_ap, scale_col, NF, op0=ALU.mult, op1=ALU.subtract), deps=[t2])
    t4 = p.op(DVE, lambda e: e.tensor_scalar(C1, DD, 0.5, None, op0=ALU.is_ge), deps=[t3])
    t5 = p.op(DVE, lambda e: e.scalar_tensor_tensor(
        QQ, DD, -0.5, C1, op0=ALU.is_lt, op1=ALU.subtract), deps=[t4])
    t6 = p.op(DVE, lambda e: e.tensor_tensor(DD, NF, QQ, op=ALU.subtract), deps=[t5])
    t7 = p.op(DVE, lambda e: e.tensor_scalar(out_bf_ap, DD, inv_col, None, op0=ALU.mult),
              deps=[t6])
    return t7



def tok_major_pair(sbuf_ap, dram_ap, ntile):
    f = dram_ap.shape[-1]
    return (sbuf_ap.rearrange("p (i f) -> p i f", i=ntile),
            dram_ap.rearrange("(i p) f -> p i f", p=128))


SC_LN = 0
SC_AMX = 64
SC_SCLX = 80
SC_INVX = 96
SC_HMAX = 112
SC_SCLH = 128
SC_INVH = 144
SC_RS = 160
SC_BCOL = 177
SC_RST = 178
SC_RSTI = 179
SC_WACC = 180
SC_MXP = 212
SC_MISC = 228
SC_TMP = 260
SC_EPS = 292


def emit_bitlinear_wquant_pass1(p, w_dram, nrow, ncol, slab_f32, ST, wacc_base, scr_out,
                                dep_gate, half_cols=None):
    cw = half_cols or ncol
    nslab = (nrow // 128) * (ncol // cw)
    toks = []
    loads = [None, None]
    readers = [None, None]
    si = 0
    for r0 in range(0, nrow, 128):
        for c0 in range(0, ncol, cw):
            b = si % 2
            ld = p.dma(SP, slab_f32[b], w_dram[r0:r0 + 128, c0:c0 + cw],
                       deps=_flat([dep_gate, readers[b]]))
            loads[b] = ld

            def facc(e, b=b, si=si):
                return e.tensor_reduce(ST[:, wacc_base + si:wacc_base + si + 1],
                                       slab_f32[b], axis=AX, op=ALU.add,
                                       apply_absolute_value=True)

            tk = p.op(DVE, facc, deps=[ld], async_=True)
            readers[b] = tk
            toks.append(tk)
            si += 1
    return toks, nslab


def emit_scalar_reduce_to_cols(p, banks, bank, ST, src_cols_ap, ones32, nelem_inv, clip_lo,
                               out_cols, deps):
    tmp = ST[:, SC_MISC + 8:SC_MISC + 9]
    t1 = p.op(DVE, lambda e: e.tensor_reduce(tmp, src_cols_ap, axis=AX, op=ALU.add),
              deps=deps, async_=True)
    war = banks.war(bank)

    def fmm(e):
        return e.matmul(banks.f32(bank)[0:1, 0:1], ones32[:, 0:1], tmp,
                        start=True, stop=True, skip_group_check=True)

    t2 = p.op(PE, fmm, deps=[t1] + war)
    banks.wrote(bank, t2)
    s11 = ST[0:1, SC_MISC + 9:SC_MISC + 10]

    def fev(e):
        return e.tensor_scalar(s11, banks.f32(bank)[0:1, 0:1], nelem_inv, clip_lo,
                               op0=ALU.mult, op1=ALU.max)

    t3 = bank_read(p, banks, bank, DVE, fev)
    toks = []
    for col_ap, fac in out_cols:
        s11b = ST[0:1, SC_MISC + 10:SC_MISC + 11]
        t4 = p.op(DVE, (lambda fac=fac, s11b=s11b: (
            lambda e: e.tensor_scalar(s11b, s11, fac, None, op0=ALU.mult)))(), deps=[t3])
        t5 = bcast_col(p, banks, bank, ones32, s11b, col_ap, deps=[t4])
        toks.append(t5)
    return toks



B0_X = 0
B0_R2 = 16384
B0_R3 = 32768
B0_WK = 40960


def emit_block0(p, env, x, Wqkv, w1, w2, y, hdram, w2t, dbg=None, upto="full"):
    AR, ST, SCR = env["AR"], env["ST"], env["SCR"]
    ONES32, ONESBF, IDB, IDF, TRI = env["ONES32"], env["ONESBF"], env["IDB"], env["IDF"], env["TRI"]
    banks = env["banks"]
    f32, bf16, i32 = arena_views(AR)
    H = 4 * D

    if upto == "ln0":
        X = f32(B0_X, NT * D)
        XN = f32(B0_R2, NT * D)
        Xs, xs = tok_major_pair(X, x, NT)
        ld_x = p.dma(SP, Xs, xs)
        import os as _os
        if _os.environ.get("LN_DELAY"):
            dl = ld_x
            for _ in range(int(_os.environ["LN_DELAY"])):
                dl = p.op(DVE, lambda e: e.tensor_copy(SCR[:, 1024:2048], SCR[:, 0:1024]),
                          deps=[dl])
            ld_x = dl
        ln_toks = emit_ln(p, X, XN, ST, SC_LN, SCR, ld_x)
        if dbg is not None:
            dbg["XN"] = (XN, _flat([ln_toks]))
            dbg["STATS"] = (env["ST"][:, 0:320], _flat([ln_toks]))
        return {"attn_toks": [], "xnt_toks": [], "ln_toks": ln_toks,
                "ld_x": ld_x, "X": X, "XN": XN, "consts": (None,) * 5,
                "last_xnt_readers": [], "work_gate": []}
    c_idb, c_idf, c_tri, c_o32, c_obf = emit_consts(p, IDB, IDF, TRI, ONES32, ONESBF)

    X = f32(B0_X, NT * D)
    XN = f32(B0_R2, NT * D)
    XNT = bf16(B0_R3, 8192)
    QT = bf16(B0_WK, 1024)
    KT = bf16(B0_WK + 1024, 1024)
    VTM = bf16(B0_WK + 2048, 1024)
    PROW = bf16(B0_WK + 3072, 1024)
    PTS = bf16(B0_WK + 4096, 512)
    WQH = bf16(B0_WK + 4608, 1536)
    STG0 = bf16(B0_WK + 0, 512)
    STG1 = bf16(B0_WK + 512, 512)

    Xs, xs = tok_major_pair(X, x, NT)
    ld_x = p.dma(SP, Xs, xs)

    ln_toks = emit_ln(p, X, XN, ST, SC_LN, SCR, ld_x)

    xnt_toks = emit_cast_transpose(
        p, banks, [XN[:, i * D:(i + 1) * D] for i in range(NT)], XNT,
        [STG0, STG1], IDB, [[ln_toks[i], c_idb] for i in range(NT)])

    if upto == "ln":
        if dbg is not None:
            dbg["XN"] = (XN, _flat([ln_toks]))
            dbg["XNTraw"] = (XNT.bitcast(F32), _flat([xnt_toks]))
            dbg["X1"] = (X, _flat([ln_toks]))
            dbg["STATS"] = (env["ST"][:, 0:320], _flat([ln_toks]))
        return {"attn_toks": [], "xnt_toks": xnt_toks, "ln_toks": ln_toks,
                "ld_x": ld_x, "X": X, "XN": XN,
                "consts": (c_idb, c_idf, c_tri, c_o32, c_obf),
                "last_xnt_readers": [], "work_gate": []}
    NHG = 8
    attn_add_toks = []
    wqh_readers = []
    qkt_readers = []
    vtm_readers = []
    prow_readers = []
    pts_readers = []
    for hg in range(NHG):
        wq_toks = []
        for k in range(8):
            for ri, j in enumerate((hg, 8 + hg, 16 + hg)):
                tok = p.dma(POOL, WQH[:, k * 384 + ri * 128:k * 384 + ri * 128 + 128],
                            Wqkv[k * 128:(k + 1) * 128, j * 128:(j + 1) * 128],
                            deps=wqh_readers if k == 0 and ri == 0 else ())
                wq_toks.append(tok)
        wqh_readers = []
        role_evacs = []
        for role in range(3):
            ev_toks = []
            for tc in range(4):
                terms = []
                for k in range(8):
                    terms.append((WQH[:, k * 384 + role * 128:k * 384 + role * 128 + 128],
                                  XNT[:, k * T + tc * 512:k * T + tc * 512 + 512],
                                  [wq_toks[k * 3 + role], xnt_toks]))
                mt = mm_group(p, banks, tc, banks.f32(tc)[:, 0:512], terms)
                wqh_readers.append(mt)
                dst = (QT, KT, PROW)[role]
                war = prow_readers if role == 2 else qkt_readers

                def fev(e, dst=dst, tc=tc, b=tc):
                    return e.tensor_copy(dst[:, tc * 512:(tc + 1) * 512],
                                         banks.f32(b)[:, 0:512])

                ev = bank_read(p, banks, tc, DVE, fev, deps=war)
                ev_toks.append(ev)
            role_evacs.append(ev_toks)
        qkt_readers = []
        prow_readers = []
        vtm_toks = []
        for kb in range(NT):
            tt_ = transpose_op(p, banks, 4 + (kb % 2),
                               banks.bf16(4 + (kb % 2))[:, 0:128],
                               PROW[:, kb * 128:(kb + 1) * 128], IDB,
                               deps=[role_evacs[2][kb // 4], c_idb])
            prow_readers.append(tt_)

            def fev(e, kb=kb, b=4 + (kb % 2)):
                return e.tensor_copy(VTM[:, kb * 128:(kb + 1) * 128],
                                     banks.bf16(b)[:, 0:128])

            vtm_toks.append(bank_read(p, banks, 4 + (kb % 2), DVE, fev,
                                      deps=vtm_readers))
        vtm_readers = []
        for h2 in range(2):
            head = hg * 2 + h2
            fo = h2 * 64
            bq_toks = []
            scr_readers = []
            for idx, SRC in enumerate((QT, KT)):
                for half in range(2):
                    def fsq3(e, SRC=SRC, half=half, fo=fo):
                        return e.activation(SCR[fo:fo + 64, 0:1024],
                                            SRC[fo:fo + 64, half * 1024:(half + 1) * 1024],
                                            AF.Square)
                    tsq = p.op(ACT, fsq3, deps=_flat([role_evacs[idx], scr_readers]))
                    qkt_readers.append(tsq)
                    scr_readers = []
                    for cc in range(2):
                        def fmm2(e, cc=cc, fo=fo):
                            return e.matmul(banks.f32(5)[0:1, 0:512],
                                            ONES32[fo:fo + 64, 0:1],
                                            SCR[fo:fo + 64, cc * 512:(cc + 1) * 512],
                                            start=True, stop=True, skip_group_check=True)
                        tm = p.op(PE, fmm2, deps=[tsq, c_o32] + banks.war(5))
                        banks.wrote(5, tm)
                        scr_readers.append(tm)

                        def fmx(e, idx=idx, half=half, cc=cc):
                            col = SC_MISC + 4 + idx * 4 + half * 2 + cc
                            return e.tensor_reduce(
                                ST[0:1, col:col + 1],
                                banks.f32(5)[0:1, 0:512], axis=AX, op=ALU.max)

                        bq_toks.append(bank_read(p, banks, 5, DVE, fmx, async_=True))
            qkt_readers.extend(scr_readers)

            def fcmb(e):
                return e.tensor_reduce(ST[0:1, SC_MISC + 12:SC_MISC + 13],
                                       ST[0:1, SC_MISC + 4:SC_MISC + 8], axis=AX, op=ALU.max)
            tq = p.op(DVE, fcmb, deps=bq_toks, async_=True)

            def fcmb2(e):
                return e.tensor_reduce(ST[0:1, SC_MISC + 13:SC_MISC + 14],
                                       ST[0:1, SC_MISC + 8:SC_MISC + 12], axis=AX, op=ALU.max)
            tk2 = p.op(DVE, fcmb2, deps=bq_toks, async_=True)
            tprod = p.op(DVE, lambda e: e.tensor_tensor(
                ST[0:1, SC_MISC + 14:SC_MISC + 15], ST[0:1, SC_MISC + 12:SC_MISC + 13],
                ST[0:1, SC_MISC + 13:SC_MISC + 14], op=ALU.mult), deps=[tq, tk2])
            tln = p.op(ACT, lambda e: e.activation(
                ST[0:1, SC_MISC + 15:SC_MISC + 16], ST[0:1, SC_MISC + 14:SC_MISC + 15],
                AF.Ln), deps=[tprod])
            texp = p.op(ACT, lambda e: e.activation(
                ST[0:1, SC_MISC + 14:SC_MISC + 15], ST[0:1, SC_MISC + 15:SC_MISC + 16],
                AF.Exp, scale=0.5), deps=[tln])
            tneg = p.op(DVE, lambda e: e.tensor_scalar(
                ST[0:1, SC_MISC + 15:SC_MISC + 16], ST[0:1, SC_MISC + 14:SC_MISC + 15],
                -0.125, None, op0=ALU.mult), deps=[texp])
            BCOL = ST[:, SC_BCOL:SC_BCOL + 1]
            tb = bcast_col(p, banks, 5, ONES32, ST[0:1, SC_MISC + 15:SC_MISC + 16],
                           BCOL, deps=[tneg])
            if dbg is not None and hg == 0 and h2 == 0:
                for nm, ap, dp in (("QT", QT.bitcast(F32), role_evacs[0]),
                                   ("KT", KT.bitcast(F32), role_evacs[1]),
                                   ("VTM", VTM.bitcast(F32), vtm_toks)):
                    tdmp = p.dma(POOL, dbg["outs"][nm], ap, deps=_flat([dp]))
                    dbg.setdefault("dump_toks", []).append(tdmp)
                    qkt_readers.append(tdmp)
                    vtm_readers.append(tdmp)
                dbg["BCOL"] = (ST[:, SC_BCOL:SC_BCOL + 1], [tb])
            for qi in range(NT):
                klen = (qi + 1) * 128
                nch = (klen + 511) // 512
                exp_toks = []
                chunk_exp = {}
                for c in range(nch):
                    cw = min(512, klen - c * 512)
                    bank = c % 2
                    terms = [(QT[fo:fo + 64, qi * 128:(qi + 1) * 128],
                              KT[fo:fo + 64, c * 512:c * 512 + cw],
                              [role_evacs[0][qi // 4], role_evacs[1][min(3, (c * 512 + cw - 1) // 512)]])]
                    ts_ = mm_group(p, banks, bank, banks.f32(bank)[:, 0:cw], terms)
                    qkt_readers.append(ts_)
                    if c == qi // 4:
                        off = qi * 128 - c * 512

                        def fmask(e, bank=bank, off=off):
                            return e.tensor_tensor(banks.f32(bank)[:, off:off + 128],
                                                   banks.f32(bank)[:, off:off + 128],
                                                   TRI[:], op=ALU.add)

                        tm2 = p.op(DVE, fmask, deps=[ts_, c_tri])
                        banks.wrote(bank, tm2)
                        ts_ = tm2

                    def fexp(e, bank=bank, c=c, cw=cw):
                        return e.activation(PROW[:, c * 512:c * 512 + cw],
                                            banks.f32(bank)[:, 0:cw], AF.Exp,
                                            bias=BCOL, scale=0.125)

                    te = bank_read(p, banks, bank, ACT, fexp,
                                   deps=_flat([ts_, tb, prow_readers]))
                    exp_toks.append(te)
                    chunk_exp[c] = te
                prow_readers = []
                trs = p.op(DVE, (lambda klen=klen: (lambda e: e.tensor_reduce(
                    ST[:, SC_RST:SC_RST + 1], PROW[:, 0:klen], axis=AX,
                    op=ALU.add)))(), deps=exp_toks, async_=True)
                prow_readers.append(trs)
                tri_ = p.op(DVE, lambda e: e.reciprocal(
                    ST[:, SC_RSTI:SC_RSTI + 1], ST[:, SC_RST:SC_RST + 1]), deps=[trs],
                    async_=True)
                pv_last = None
                group_tr = []
                for kb in range(qi + 1):
                    sl = kb % 8
                    ttr = transpose_op(p, banks, 6 + (kb // 8) % 2,
                                       banks.bf16(6 + (kb // 8) % 2)[:, sl * 128:(sl + 1) * 128],
                                       PROW[:, kb * 128:(kb + 1) * 128], IDB,
                                       deps=[chunk_exp[(kb * 128) // 512], c_idb])
                    prow_readers.append(ttr)
                    group_tr.append(ttr)
                    if sl == 7 or kb == qi:
                        g = kb // 8
                        nsl = sl + 1
                        bnk = 6 + g % 2

                        def fev2(e, nsl=nsl, bnk=bnk):
                            return e.tensor_copy(PTS[:, 0:nsl * 128],
                                                 banks.bf16(bnk)[:, 0:nsl * 128])

                        tev = bank_read(p, banks, bnk, DVE, fev2,
                                        deps=_flat([group_tr, pts_readers]))
                        pts_readers = []
                        group_tr = []
                        for kb2 in range(g * 8, kb + 1):
                            def fpv(e, kb2=kb2, s=(kb2 == 0), st=(kb2 == qi), fo=fo):
                                return e.matmul(banks.f32(3)[:, 0:64],
                                                PTS[:, (kb2 % 8) * 128:(kb2 % 8 + 1) * 128],
                                                VTM[:, kb2 * 128 + fo:kb2 * 128 + fo + 64],
                                                start=s, stop=st, skip_group_check=True)

                            deps = _flat([tev, vtm_toks[kb2]]) + (banks.war(3) if kb2 == 0 else [])
                            pv_last = p.op(PE, fpv, deps=deps)
                            pts_readers.append(pv_last)
                            vtm_readers.append(pv_last)
                banks.wrote(3, pv_last)
                Xsl = X[:, qi * D + head * 64:qi * D + head * 64 + 64]

                def fadd(e, Xsl=Xsl):
                    return e.scalar_tensor_tensor(Xsl, banks.f32(3)[:, 0:64],
                                                  ST[:, SC_RSTI:SC_RSTI + 1], Xsl,
                                                  op0=ALU.mult, op1=ALU.add)

                if dbg is not None and hg == 0 and h2 == 0 and qi == 3:
                    def fpvd(e):
                        return e.tensor_copy(SCR[:, 1024:1088], banks.f32(3)[:, 0:64])
                    tpv = bank_read(p, banks, 3, DVE, fpvd)
                    tdmp2 = p.dma(POOL, dbg["outs"]["PV"], SCR[:, 1024:1088], deps=[tpv])
                    dbg.setdefault("dump_toks", []).append(tdmp2)
                    tdmp3 = p.dma(POOL, dbg["outs"]["PTS"], PTS.bitcast(F32),
                                  deps=_flat([pts_readers]))
                    dbg.setdefault("dump_toks", []).append(tdmp3)
                    pts_readers.append(tdmp3)
                tadd = bank_read(p, banks, 3, DVE, fadd, deps=[tri_, ld_x])
                attn_add_toks.append(tadd)
                if dbg is not None and hg == 0 and h2 == 0 and qi == 3:
                    dbg["PROW"] = (PROW.bitcast(F32), _flat([exp_toks]))
                    dbg["RSTI"] = (ST[:, SC_RSTI:SC_RSTI + 1], [tri_])
                    for nm in ("PROW", "RSTI"):
                        ap, dp = dbg[nm]
                        od = dbg["outs"][nm]
                        tdmp = p.dma(POOL, od, ap, deps=dp)
                        dbg.setdefault("dump_toks", []).append(tdmp)
                        prow_readers.append(tdmp)
    if dbg is not None:
        dbg["XN"] = (XN, _flat([ln_toks]))
        dbg["XNTraw"] = (XNT.bitcast(F32), _flat([xnt_toks]))
        dbg["X1"] = (X, _flat([attn_add_toks]))
    work_gate = _flat([qkt_readers, vtm_readers, prow_readers, pts_readers,
                       wqh_readers])
    return {"attn_toks": attn_add_toks, "xnt_toks": xnt_toks, "ln_toks": ln_toks,
            "ld_x": ld_x, "X": X, "XN": XN, "consts": (c_idb, c_idf, c_tri, c_o32, c_obf),
            "last_xnt_readers": list(wqh_readers), "work_gate": work_gate}


def emit_act_quant(p, env, st0, XQT, scr_base, work_gate, xnt_war):
    AR, ST = env["AR"], env["ST"]
    IDB = env["IDB"]
    banks = env["banks"]
    f32, bf16, i32 = arena_views(AR)
    XN = st0["XN"]
    c_idb = st0["consts"][0]
    NI = i32(scr_base + 0, 1024)
    NF = f32(scr_base + 1024, 1024)
    DD = f32(scr_base + 2048, 1024)
    C1 = f32(scr_base + 3072, 1024)
    QQ = f32(scr_base + 4096, 1024)
    scr_q = {"NI": NI, "NF": NF, "DD": DD, "C1": C1, "QQ": QQ}
    STGQ = [bf16(scr_base + 5120, 512), bf16(scr_base + 5632, 512)]
    AMX = ST[:, SC_AMX:SC_AMX + NT]
    SCLX = ST[:, SC_SCLX:SC_SCLX + NT]
    INVX = ST[:, SC_INVX:SC_INVX + NT]
    amx_toks = []
    for i in range(NT):
        def fmx(e, i=i):
            return e.tensor_reduce(AMX[:, i:i + 1], XN[:, i * D:(i + 1) * D], axis=AX,
                                   op=ALU.max, apply_absolute_value=True)
        amx_toks.append(p.op(DVE, fmx, deps=[st0["ln_toks"][i]], async_=True))
    TMPB = ST[:, SC_TMP:SC_TMP + 16]
    t1 = p.op(DVE, lambda e: e.tensor_scalar(TMPB, AMX, 1e-5, None, op0=ALU.max),
              deps=amx_toks)
    t2 = p.op(DVE, lambda e: e.tensor_scalar(INVX, TMPB, 1.0 / 127.0, None, op0=ALU.mult),
              deps=[t1])
    t3 = p.op(DVE, lambda e: e.reciprocal(SCLX, TMPB), deps=[t1], async_=True)
    t4 = p.op(DVE, lambda e: e.tensor_scalar(SCLX, SCLX, 127.0, None, op0=ALU.mult),
              deps=[t3])
    xn_readers = list(amx_toks)

    def produce_q(i, stg_ap, pdeps):
        tok = emit_quant_rounding(p, XN[:, i * D:(i + 1) * D], SCLX[:, i:i + 1],
                                  INVX[:, i:i + 1], stg_ap, scr_q,
                                  deps=_flat([pdeps, t2, t4, work_gate]))
        xn_readers.append(tok)
        return tok

    xqt_toks = emit_cast_transpose(
        p, banks, [None] * NT, XQT, STGQ, IDB,
        [[c_idb] for _ in range(NT)], war_dst=xnt_war, produce=produce_q)
    return xqt_toks, xn_readers


def emit_bitlinear_ffn(p, env, st0, mm1_ws, w2, y, hdram, w2t, mode="gelu"):
    AR, ST, SCR = env["AR"], env["ST"], env["SCR"]
    ONES32, IDB, IDF = env["ONES32"], env["IDB"], env["IDF"]
    banks = env["banks"]
    f32, bf16, i32 = arena_views(AR)
    X, XN = st0["X"], st0["XN"]
    c_idb, c_idf, c_tri, c_o32, c_obf = st0["consts"]
    H = 4 * D
    work_gate = st0["work_gate"]
    xnt_war = st0["last_xnt_readers"]

    XQT = bf16(B0_R3, 8192)
    if st0.get("xq_pre") is not None:
        xqt_toks, xn_readers = st0["xq_pre"]
    else:
        xqt_toks, xn_readers = emit_act_quant(p, env, st0, XQT, B0_WK, work_gate,
                                              xnt_war)

    S0 = f32(B0_R2 + 0, 1024)
    S1 = f32(B0_R2 + 1024, 1024)
    A2B = bf16(B0_R2 + 2048, 512)
    B2B = bf16(B0_R2 + 2560, 512)
    WQS = bf16(B0_R2 + 3072, 512)
    W1T = [bf16(B0_R2 + 3584, 512), bf16(B0_R2 + 4096, 512)]
    HG = f32(B0_R2 + 4608, 2048)
    HTMB = [bf16(B0_R2 + 6656, 1024), bf16(B0_R2 + 7680, 1024)]

    dbg = st0.get("dbg")
    scol_base = [SC_MISC + 0, SC_MISC + 12]
    wcols = []
    s1_toks = []
    prev = xn_readers
    for wi, w_ in enumerate(mm1_ws):
        pw, _ = emit_bitlinear_wquant_pass1(
            p, w_, 4 * D, D, [S0, S1], ST, SC_WACC, SCR, dep_gate=prev)
        cb = scol_base[wi]
        cols = (ST[:, cb:cb + 1], ST[:, cb + 1:cb + 2], ST[:, cb + 2:cb + 3])
        stoks = emit_scalar_reduce_to_cols(
            p, banks, 5, ST, ST[:, SC_WACC:SC_WACC + 32], ONES32,
            1.0 / (4096.0 * 1024.0), 1e-5,
            [(cols[0], 1.0), (cols[1], 0.5), (cols[2], -0.5)], deps=pw)
        wcols.append(cols)
        s1_toks.extend(stoks)
        prev = _flat([stoks, pw])

    if dbg is not None and "S1" in dbg["outs"]:
        tdS = p.dma(POOL, dbg["outs"]["S1"], ST[:, SC_MISC:SC_MISC + 8], deps=s1_toks)
        dbg.setdefault("dump_toks", []).append(tdS)
        tdX = p.dma(POOL, dbg["outs"]["XQT"], XQT.bitcast(F32), deps=xqt_toks)
        dbg.setdefault("dump_toks", []).append(tdX)
    HMAX = ST[:, SC_HMAX:SC_HMAX + NT]
    thm0 = p.op(DVE, lambda e: e.memset(HMAX, 0.0))
    slab_rd = [s1_toks[-2:], s1_toks[-2:]]
    hdump_toks = []
    htmb_rd = [[], []]
    hmax_prev = thm0
    w1t_rd = [[], []]
    for s in range(32):
        g_toks = []
        for wi, w_ in enumerate(mm1_ws):
            b = (s * len(mm1_ws) + wi) % 2
            SLB = S0 if b == 0 else S1
            C0, C1_, C2 = wcols[wi]
            ld = p.dma(SP, SLB, w_[s * 128:(s + 1) * 128, :],
                       deps=_flat([slab_rd[b], s1_toks]))
            ta = p.op(DVE, (lambda SLB=SLB, C1_=C1_, C0=C0: (lambda e: e.tensor_scalar(
                A2B, SLB, C1_, C0, op0=ALU.is_ge, op1=ALU.mult)))(), deps=[ld])
            tb_ = p.op(DVE, (lambda SLB=SLB, C2=C2, C0=C0: (lambda e: e.tensor_scalar(
                B2B, SLB, C2, C0, op0=ALU.is_le, op1=ALU.mult)))(), deps=[ld])
            tw = p.op(DVE, lambda e: e.tensor_tensor(WQS, A2B, B2B, op=ALU.subtract),
                      deps=[ta, tb_])
            slab_rd[b] = [ta, tb_]
            trs = []
            for k in range(8):
                trs.append(transpose_op(p, banks, 4,
                                        banks.bf16(4)[:, k * 128:(k + 1) * 128],
                                        WQS[:, k * 128:(k + 1) * 128], IDB,
                                        deps=[tw, c_idb]))
            tev = bank_read(p, banks, 4, DVE,
                            (lambda wi=wi: (lambda e: e.tensor_copy(
                                W1T[wi][:], banks.bf16(4)[:, 0:1024])))(),
                            deps=_flat([trs, w1t_rd[wi]]))
            mm_toks = []
            for tc in range(4):
                terms = [(W1T[wi][:, k * 128:(k + 1) * 128],
                          XQT[:, k * T + tc * 512:k * T + tc * 512 + 512],
                          [tev, xqt_toks]) for k in range(8)]
                mm_toks.append(mm_group(p, banks, tc, banks.f32(tc)[:, 0:512], terms))
            w1t_rd[wi] = mm_toks
            if mode == "gelu":
                for tc in range(4):
                    def fg(e, tc=tc):
                        return e.activation(HG[:, tc * 512:(tc + 1) * 512],
                                            banks.f32(tc)[:, 0:512], AF.Gelu)
                    g_toks.append(bank_read(p, banks, tc, ACT, fg, deps=[mm_toks[tc]]))
            elif wi == 0:
                sg = []
                for tc in range(4):
                    def fg(e, tc=tc):
                        return e.activation(HG[:, tc * 512:(tc + 1) * 512],
                                            banks.f32(tc)[:, 0:512], AF.Silu)
                    sg.append(bank_read(p, banks, tc, ACT, fg, deps=[mm_toks[tc]]))
                g_toks = sg
            else:
                ng = []
                for tc in range(4):
                    def fu(e, tc=tc):
                        return e.tensor_tensor(HG[:, tc * 512:(tc + 1) * 512],
                                               HG[:, tc * 512:(tc + 1) * 512],
                                               banks.f32(tc)[:, 0:512], op=ALU.mult)
                    ng.append(bank_read(p, banks, tc, DVE, fu,
                                        deps=[mm_toks[tc], g_toks[tc]]))
                g_toks = ng
        tr_toks = []
        for i in range(NT):
            bnk = i // 4
            tr_toks.append(transpose_op(
                p, banks, bnk, banks.f32(bnk)[:, (i % 4) * 128:(i % 4 + 1) * 128],
                HG[:, i * 128:(i + 1) * 128], IDF, deps=[g_toks[bnk], c_idf]))
        mx_toks = []
        for i in range(NT):
            bnk = i // 4

            def fmx2(e, i=i, bnk=bnk):
                return e.tensor_reduce(ST[:, SC_MXP + i:SC_MXP + i + 1],
                                       banks.f32(bnk)[:, (i % 4) * 128:(i % 4 + 1) * 128],
                                       axis=AX, op=ALU.max, apply_absolute_value=True)
            mx_toks.append(bank_read(p, banks, bnk, DVE, fmx2, deps=[tr_toks[i]],
                                     async_=True))
        hmax_prev = p.op(DVE, lambda e: e.tensor_tensor(HMAX, HMAX,
                                                        ST[:, SC_MXP:SC_MXP + NT],
                                                        op=ALU.max),
                         deps=_flat([mx_toks, hmax_prev]))
        ev_toks = []
        for bnk in range(4):
            def fev(e, bnk=bnk, b=b):
                return e.tensor_copy(HTMB[b][:, bnk * 512:(bnk + 1) * 512],
                                     banks.f32(bnk)[:, 0:512])
            ev_toks.append(bank_read(p, banks, bnk, DVE, fev,
                                     deps=_flat([tr_toks[bnk * 4:(bnk + 1) * 4], htmb_rd[b]])))
        _hs, _hd = tok_major_pair(HTMB[b], hdram[:, s * 128:(s + 1) * 128], NT)
        dmp = p.dma(POOL, _hd, _hs, deps=ev_toks)
        htmb_rd[b] = [dmp]
        hdump_toks.append(dmp)
        if dbg is not None and s == 0 and "HROW" in dbg["outs"]:
            tdh = p.dma(POOL, dbg["outs"]["HROW"], HTMB[b].bitcast(F32), deps=ev_toks)
            dbg.setdefault("dump_toks", []).append(tdh)
            htmb_rd[b].append(tdh)

    TMPH = ST[:, SC_TMP:SC_TMP + 16]
    h1 = p.op(DVE, lambda e: e.tensor_scalar(TMPH, HMAX, 1e-5, None, op0=ALU.max),
              deps=[hmax_prev])
    SCLH = ST[:, SC_SCLH:SC_SCLH + NT]
    INVH = ST[:, SC_INVH:SC_INVH + NT]
    h2 = p.op(DVE, lambda e: e.tensor_scalar(INVH, TMPH, 1.0 / 127.0, None, op0=ALU.mult),
              deps=[h1])
    h3 = p.op(DVE, lambda e: e.reciprocal(SCLH, TMPH), deps=[h1], async_=True)
    h4 = p.op(DVE, lambda e: e.tensor_scalar(SCLH, SCLH, 127.0, None, op0=ALU.mult),
              deps=[h3])

    W2A = f32(B0_R2 + 8704, 2048)
    W2B = f32(B0_R2 + 10752, 2048)
    W2TRN = bf16(B0_R2 + 12800, 1024)
    A2B2 = bf16(B0_R2 + 13824, 1024)
    B2B2 = bf16(B0_R2 + 14848, 1024)
    TST8 = bf16(B0_R2 + 15872, 512)
    p2_toks, _ = emit_bitlinear_wquant_pass1(
        p, w2, D, 4 * D, [W2A, W2B], ST, SC_WACC, SCR,
        dep_gate=s1_toks, half_cols=2048)
    S2COL = ST[:, SC_MISC + 4:SC_MISC + 5]
    HS2COL = ST[:, SC_MISC + 5:SC_MISC + 6]
    NHS2COL = ST[:, SC_MISC + 6:SC_MISC + 7]
    s2_toks = emit_scalar_reduce_to_cols(
        p, banks, 5, ST, ST[:, SC_WACC:SC_WACC + 16], ONES32, 1.0 / (1024.0 * 4096.0),
        1e-5, [(S2COL, 1.0), (HS2COL, 0.5), (NHS2COL, -0.5)], deps=p2_toks)
    w2slab_rd = [p2_toks[-2:], p2_toks[-2:]]
    w2t_dumps = []
    tst_rd = []
    for s in range(8):
        for hh in range(2):
            si = s * 2 + hh
            b = si % 2
            SLB = W2A if b == 0 else W2B
            ld = p.dma(SP, SLB, w2[s * 128:(s + 1) * 128, hh * 2048:(hh + 1) * 2048],
                       deps=_flat([w2slab_rd[b], s2_toks]))
            ta = p.op(DVE, (lambda SLB=SLB: (lambda e: e.tensor_scalar(
                A2B2, SLB, HS2COL, S2COL, op0=ALU.is_ge, op1=ALU.mult)))(), deps=[ld])
            tb_ = p.op(DVE, (lambda SLB=SLB: (lambda e: e.tensor_scalar(
                B2B2, SLB, NHS2COL, S2COL, op0=ALU.is_le, op1=ALU.mult)))(), deps=[ld])
            tw = p.op(DVE, lambda e: e.tensor_tensor(W2TRN, A2B2, B2B2, op=ALU.subtract),
                      deps=[ta, tb_])
            w2slab_rd[b] = [ta, tb_]
            for grp in range(2):
                trs = []
                for kk in range(8):
                    c = hh * 16 + grp * 8 + kk
                    trs.append(transpose_op(
                        p, banks, 6, banks.bf16(6)[:, kk * 128:(kk + 1) * 128],
                        W2TRN[:, (grp * 8 + kk) * 128:(grp * 8 + kk + 1) * 128], IDB,
                        deps=[tw, c_idb]))
                tev = bank_read(p, banks, 6, DVE,
                                lambda e: e.tensor_copy(TST8[:], banks.bf16(6)[:, 0:1024]),
                                deps=_flat([trs, tst_rd]))
                c0 = hh * 16 + grp * 8
                _ws, _wd = tok_major_pair(TST8[:], w2t[c0 * 128:(c0 + 8) * 128,
                                                         s * 128:(s + 1) * 128], 8)
                dmp = p.dma(POOL, _wd, _ws, deps=[tev])
                tst_rd = [dmp]
                w2t_dumps.append(dmp)

    HQT = bf16(B0_R2 + 0, 8192)
    HLOAD = [bf16(B0_R2 + 8192, 2048), bf16(B0_R2 + 10240, 2048)]
    HQ = [bf16(B0_R2 + 12288, 2048), bf16(B0_R2 + 14336, 2048)]
    W2S = [bf16(B0_WK + 5120, 512), bf16(B0_WK + 5632, 512)]
    NI = i32(B0_WK + 0, 1024)
    NF = f32(B0_WK + 1024, 1024)
    DD = f32(B0_WK + 2048, 1024)
    C1 = f32(B0_WK + 3072, 1024)
    QQ = f32(B0_WK + 4096, 1024)
    r2_gate = _flat([hdump_toks, w2t_dumps, s2_toks, [w2slab_rd[0], w2slab_rd[1]], tst_rd])
    hload_rd = [r2_gate, r2_gate]
    hq_rd = [r2_gate, r2_gate]
    hqt_rd = []
    w2s_rd = [_flat([xqt_toks]), _flat([xqt_toks])]
    add_toks = []
    for tc in range(4):
        hqt_evs = []
        for ti in range(4):
            i = tc * 4 + ti
            b = i % 2
            ldh = p.dma(SP, HLOAD[b], hdram[i * 128:(i + 1) * 128, :],
                        deps=_flat([hload_rd[b]]))
            q_toks = []
            for cc in range(4):
                scr_cc = {"NI": NI, "NF": NF, "DD": DD, "C1": C1, "QQ": QQ}
                tq = emit_quant_rounding(
                    p, HLOAD[b][:, cc * 1024:(cc + 1) * 1024], SCLH[:, i:i + 1],
                    INVH[:, i:i + 1], HQ[b][:, cc * 1024:(cc + 1) * 1024], scr_cc,
                    deps=_flat([ldh, h2, h4, hq_rd[b]]))
                q_toks.append(tq)
            hload_rd[b] = q_toks
            trs_all = []
            for k in range(32):
                bnk = k % 8
                sl = k // 8
                ttr = transpose_op(p, banks, bnk,
                                   banks.bf16(bnk)[:, sl * 128:(sl + 1) * 128],
                                   HQ[b][:, k * 128:(k + 1) * 128], IDB,
                                   deps=[q_toks[k // 8 if False else k // 8], c_idb])
                trs_all.append(ttr)
            hq_rd[b] = trs_all
            for k in range(32):
                bnk = k % 8
                sl = k // 8

                def fev3(e, k=k, ti=ti, bnk=bnk, sl=sl):
                    return e.tensor_copy(HQT[:, k * 512 + ti * 128:k * 512 + ti * 128 + 128],
                                         banks.bf16(bnk)[:, sl * 128:(sl + 1) * 128])
                hqt_evs.append(bank_read(p, banks, bnk, DVE, fev3,
                                         deps=_flat([trs_all[k], hqt_rd])))
        hqt_rd = []
        mm_last = [None] * 8
        for k in range(32):
            b = k % 2
            ldw = p.dma(SP, W2S[b], w2t[k * 128:(k + 1) * 128, :],
                        deps=_flat([w2s_rd[b], w2t_dumps]))
            new_rd = []
            for ti in range(4):
                for half in range(2):
                    bnk = ti * 2 + half

                    def fm(e, k=k, ti=ti, half=half, bnk=bnk, b=b):
                        return e.matmul(banks.f32(bnk)[:, 0:512],
                                        HQT[:, k * 512 + ti * 128:k * 512 + ti * 128 + 128],
                                        W2S[b][:, half * 512:(half + 1) * 512],
                                        start=(k == 0), stop=(k == 31),
                                        skip_group_check=True)
                    deps = _flat([ldw, hqt_evs]) + (banks.war(bnk) if k == 0 else [])
                    tokm = p.op(PE, fm, deps=deps)
                    new_rd.append(tokm)
                    if k == 31:
                        banks.wrote(bnk, tokm)
                        mm_last[bnk] = tokm
            w2s_rd[b] = new_rd
        for ti in range(4):
            i = tc * 4 + ti
            for half in range(2):
                bnk = ti * 2 + half
                Xsl = X[:, i * D + half * 512:i * D + half * 512 + 512]

                def fadd2(e, Xsl=Xsl, bnk=bnk):
                    return e.tensor_tensor(Xsl, Xsl, banks.f32(bnk)[:, 0:512], op=ALU.add)

                tadd = bank_read(p, banks, bnk, DVE, fadd2,
                                 deps=_flat([mm_last[bnk], st0["attn_toks"]]))
                add_toks.append(tadd)
                hqt_rd.append(mm_last[bnk])

    if dbg is not None and "STATS2" in dbg["outs"]:
        tdS2 = p.dma(POOL, dbg["outs"]["STATS2"], ST[:, 0:320],
                     deps=_flat([add_toks, h2, h4]))
        dbg.setdefault("dump_toks", []).append(tdS2)
    _ys, _yd = tok_major_pair(X, y, NT)
    st_tok = p.dma(POOL, _yd, _ys, deps=_flat([add_toks, st0["attn_toks"]]))
    p.op(POOL, lambda e: e.nop(), deps=[st_tok])
    return st_tok


def build_block0(debug=False, upto="full"):
    nc = bass.Bass("TRN2", target_bir_lowering=False)
    x = nc.dram_tensor("x", [T, D], F32, kind="ExternalInput").ap()
    Wqkv = nc.dram_tensor("W_qkv", [D, 3 * D], F32, kind="ExternalInput").ap()
    w1 = nc.dram_tensor("w1", [4 * D, D], F32, kind="ExternalInput").ap()
    w2 = nc.dram_tensor("w2", [D, 4 * D], F32, kind="ExternalInput").ap()
    y = nc.dram_tensor("y", [T, D], F32, kind="ExternalOutput").ap()
    hdram = nc.dram_tensor("hdram", [T, 4 * D], BF16, kind="Internal").ap()
    w2t = nc.dram_tensor("w2t", [4 * D, D], BF16, kind="Internal").ap()
    env = std_env(nc, 47104)
    p = Prog(nc)
    dbg = None
    if debug:
        shapes = {"XN": [128, NT * D], "XNTraw": [128, 8192], "X1": [128, NT * D],
                  "QT": [128, 1024], "KT": [128, 1024], "VTM": [128, 1024],
                  "BCOL": [128, 1], "PROW": [128, 1024], "RS": [128, 17],
                  "RSTI": [128, 1], "STATS": [128, 320],
                  "PV": [128, 64], "PTS": [128, 512], "S1": [128, 8],
                  "XQT": [128, 8192], "HROW": [128, 1024], "STATS2": [128, 320]}
        outs = {nm: nc.dram_tensor("d_" + nm, sh, F32, kind="ExternalOutput").ap()
                for nm, sh in shapes.items()}
        dbg = {"outs": outs}
    st0 = emit_block0(p, env, x, Wqkv, w1, w2, y, hdram, w2t, dbg=dbg, upto=upto)
    st0["dbg"] = dbg
    if debug:
        for nm in ("XN", "XNTraw", "X1", "BCOL", "STATS"):
            if nm in dbg:
                ap, dp = dbg[nm]
                dbg.setdefault("dump_toks", []).append(
                    p.dma(POOL, dbg["outs"][nm], ap, deps=dp))
    if upto == "full":
        emit_bitlinear_ffn(p, env, st0, [w1], w2, y, hdram, w2t, mode="gelu")
    else:
        fin = _flat([dbg.get("dump_toks", []) if dbg else [], st0["attn_toks"]])
        p.op(POOL, lambda e: e.nop(), deps=fin)
    p.emit(env["sems"])
    return nc



def emit_block1(p, env, x, Wq, Wk, Wv, Wo, bo, y, dbg=None):
    AR, ST, SCR = env["AR"], env["ST"], env["SCR"]
    ONES32, IDB, IDF = env["ONES32"], env["IDB"], env["IDF"]
    banks = env["banks"]
    f32, bf16, i32 = arena_views(AR)
    c_idb, c_idf, c_tri, c_o32, c_obf = emit_consts(
        p, IDB, IDF, env["TRI"], ONES32, env["ONESBF"])

    X = f32(B0_X, NT * D)
    XN = f32(B0_R2, NT * D)
    XQT = bf16(B0_R3, 8192)
    XNTL = bf16(B0_WK + 0, 4096)
    QTb = bf16(B0_R2 + 0, 1024)
    KTb = bf16(B0_R2 + 1024, 1024)
    VTM = bf16(B0_R2 + 2048, 4096)
    AH = bf16(B0_R2 + 6144, 4096)
    PROW = bf16(B0_R2 + 10240, 1024)
    PTS = bf16(B0_R2 + 11264, 512)
    BOBC = f32(B0_R2 + 15360, 1024)

    st = {"X": X, "XN": XN, "consts": (c_idb, c_idf, c_tri, c_o32, c_obf)}

    Xs, xs = tok_major_pair(X, x, NT)
    ld_x = p.dma(SP, Xs, xs)
    st["ld_x"] = ld_x
    ln_toks = emit_ln(p, X, XN, ST, SC_LN, SCR, ld_x)
    st["ln_toks"] = ln_toks

    xqt_toks, xn_readers = emit_act_quant(p, env, st, XQT, B0_WK, [], [])

    STGL = [bf16(B0_R2 + 14848, 256), bf16(B0_R2 + 15104, 256)]
    xntl_toks = emit_cast_transpose(
        p, banks, [XN[:, i * D:i * D + 512] for i in range(NT)], XNTL,
        STGL, IDB, [_flat([ln_toks[i], c_idb, xn_readers]) for i in range(NT)],
        war_dst=xn_readers, width=512)

    STGW = [bf16(B0_R2 + 14848, 256), bf16(B0_R2 + 15104, 256)]

    def wqt_view(wi):
        return bf16(B0_R2 + 11776 + wi * 1024, 1024)

    wev = {}
    for wi, wsrc in enumerate((Wq, Wk, Wv)):
        for srow in range(4):
            b = srow % 2
            ldw = p.dma(POOL, STGW[b], wsrc[srow * 128:(srow + 1) * 128, :],
                        deps=_flat([xntl_toks, wev.get((wi, srow - 2))]))
            trs = [transpose_op(p, banks, 5, banks.bf16(5)[:, kc * 128:(kc + 1) * 128],
                                STGW[b][:, kc * 128:(kc + 1) * 128], IDB,
                                deps=[ldw, c_idb])
                   for kc in range(4)]

            def fev(e, wi=wi, srow=srow):
                dst = wqt_view(wi).rearrange("p (k o) -> p k o", k=4)[
                    :, :, srow * 128:(srow + 1) * 128]
                srcv = banks.bf16(5)[:, 0:512].rearrange("p (k o) -> p k o", k=4)
                return e.tensor_copy(dst, srcv)

            wev[(wi, srow)] = bank_read(p, banks, 5, DVE, fev, deps=trs)
    wq_toks = [wev[(0, r)] for r in range(4)]
    wk_toks = [wev[(1, r)] for r in range(4)]
    wv_toks = [wev[(2, r)] for r in range(4)]

    qkt_readers = []
    vtm_toks = []
    for ti in range(NT):
        bank = 4 + ti % 2
        terms = [(XNTL[:, kc * T + ti * 128:kc * T + ti * 128 + 128],
                  wqt_view(2)[:, kc * 512:(kc + 1) * 512],
                  [wv_toks, xntl_toks]) for kc in range(4)]
        mt = mm_group(p, banks, bank, banks.f32(bank)[:, 0:512], terms)

        def fev3(e, ti=ti, bank=bank):
            return e.tensor_copy(VTM[:, ti * 512:(ti + 1) * 512],
                                 banks.f32(bank)[:, 0:512])

        vtm_toks.append(bank_read(p, banks, bank, DVE, fev3))

    ah_toks = []
    prow_readers = []
    pts_readers = []
    qk_rd = []
    for ch in range(4):
        role_ev = {}
        for role, wtoks, DST in ((0, wq_toks, QTb), (1, wk_toks, KTb)):
            evs = []
            for tc in range(4):
                bank = tc
                terms = [(wqt_view(role)[:, kc * 512 + ch * 128:kc * 512 + ch * 128 + 128],
                          XNTL[:, kc * T + tc * 512:kc * T + tc * 512 + 512],
                          [wtoks, xntl_toks]) for kc in range(4)]
                mt = mm_group(p, banks, bank, banks.f32(bank)[:, 0:512], terms)

                def fev2(e, DST=DST, tc=tc, bank=bank):
                    return e.tensor_copy(DST[:, tc * 512:(tc + 1) * 512],
                                         banks.f32(bank)[:, 0:512])

                evs.append(bank_read(p, banks, bank, DVE, fev2, deps=qk_rd))
            role_ev[role] = evs
        qk_rd = []
        bnd_toks = {}
        sq_rd = []
        for role in (0, 1):
            SRCQ = (QTb, KTb)[role]

            def fsq(e, SRCQ=SRCQ):
                return e.activation(SCR[:, 0:2048], SRCQ[:, 0:2048], AF.Square)

            tsq = p.op(ACT, fsq, deps=_flat([role_ev[role], sq_rd]), async_=True)
            qk_rd.append(tsq)
            sq_rd = []
            for hh in range(4):
                base = hh * 32
                mparts = []
                for tc in range(4):
                    def fmm(e, base=base, tc=tc):
                        return e.matmul(banks.f32(5)[0:1, 0:512],
                                        ONES32[base:base + 32, 0:1],
                                        SCR[base:base + 32, tc * 512:(tc + 1) * 512],
                                        start=True, stop=True, skip_group_check=True,
                                        tile_position=(base, 0))

                    tm = p.op(PE, fmm, deps=[tsq, c_o32] + banks.war(5))
                    banks.wrote(5, tm)
                    sq_rd.append(tm)

                    def fmx(e, tc=tc):
                        col = SC_TMP + tc
                        return e.tensor_reduce(ST[0:1, col:col + 1],
                                               banks.f32(5)[0:1, 0:512],
                                               axis=AX, op=ALU.max)

                    mparts.append(bank_read(p, banks, 5, DVE, fmx, async_=True))

                def fcmb(e, role=role, hh=hh):
                    return e.tensor_reduce(
                        ST[0:1, SC_WACC + role * 16 + hh:SC_WACC + role * 16 + hh + 1],
                        ST[0:1, SC_TMP:SC_TMP + 4], axis=AX, op=ALU.max)

                bnd_toks[(role, hh)] = p.op(DVE, fcmb, deps=mparts, async_=True)

        if dbg is not None and ch == 0:
            for nm, ap, dp in (("QTB", QTb.bitcast(F32), role_ev[0]),
                               ("KTB", KTb.bitcast(F32), role_ev[1]),
                               ("VTMB", VTM.bitcast(F32), vtm_toks),
                               ("XNTL", XNTL.bitcast(F32), xntl_toks),
                               ("BND", ST[:, 0:320], list(bnd_toks.values()))):
                td = p.dma(POOL, dbg["outs"][nm], ap, deps=_flat([dp]))
                dbg.setdefault("dump_toks", []).append(td)
                qk_rd.append(td)
                vtm_toks = list(vtm_toks) + [td]
        for hh in range(4):
            h = ch * 4 + hh
            base = hh * 32
            tprod = p.op(DVE, (lambda hh=hh: (lambda e: e.tensor_tensor(
                ST[0:1, SC_TMP:SC_TMP + 1], ST[0:1, SC_WACC + hh:SC_WACC + hh + 1],
                ST[0:1, SC_WACC + 16 + hh:SC_WACC + 16 + hh + 1], op=ALU.mult)))(),
                deps=_flat([bnd_toks[(0, hh)], bnd_toks[(1, hh)], sq_rd]), async_=True)
            tln = p.op(ACT, lambda e: e.activation(
                ST[0:1, SC_TMP + 1:SC_TMP + 2], ST[0:1, SC_TMP:SC_TMP + 1], AF.Ln),
                deps=[tprod], async_=True)
            texp = p.op(ACT, lambda e: e.activation(
                ST[0:1, SC_TMP:SC_TMP + 1], ST[0:1, SC_TMP + 1:SC_TMP + 2], AF.Exp,
                scale=0.5), deps=[tln], async_=True)
            tneg = p.op(DVE, lambda e: e.tensor_scalar(
                ST[0:1, SC_TMP + 1:SC_TMP + 2], ST[0:1, SC_TMP:SC_TMP + 1], -0.125,
                None, op0=ALU.mult), deps=[texp], async_=True)
            BCOL = ST[:, SC_BCOL:SC_BCOL + 1]
            tb = bcast_col(p, banks, 5, ONES32, ST[0:1, SC_TMP + 1:SC_TMP + 2], BCOL,
                           deps=[tneg])
            for qi in range(NT):
                exp_toks = []
                for c in range(4):
                    bank = c % 2
                    war = banks.war(bank)

                    def fs(e, base=base, qi=qi, c=c, bank=bank):
                        return e.matmul(banks.f32(bank)[:, 0:512],
                                        QTb[base:base + 32, qi * 128:qi * 128 + 128],
                                        KTb[base:base + 32, c * 512:c * 512 + 512],
                                        start=True, stop=True, skip_group_check=True,
                                        tile_position=(base, 0))

                    ts_ = p.op(PE, fs, deps=_flat([role_ev[0], role_ev[1]]) + war)
                    banks.wrote(bank, ts_)
                    qk_rd.append(ts_)
                    qkt_readers.append(ts_)

                    def fexp(e, bank=bank, c=c):
                        return e.activation(PROW[:, c * 512:(c + 1) * 512],
                                            banks.f32(bank)[:, 0:512], AF.Exp,
                                            bias=BCOL, scale=0.125)

                    te = bank_read(p, banks, bank, ACT, fexp,
                                   deps=_flat([ts_, tb, prow_readers]))
                    exp_toks.append(te)
                prow_readers = []
                trs = p.op(DVE, lambda e: e.tensor_reduce(
                    ST[:, SC_RST:SC_RST + 1], PROW[:, 0:2048], axis=AX, op=ALU.add),
                    deps=exp_toks, async_=True)
                prow_readers.append(trs)
                tri_ = p.op(DVE, lambda e: e.reciprocal(
                    ST[:, SC_RSTI:SC_RSTI + 1], ST[:, SC_RST:SC_RST + 1]),
                    deps=[trs], async_=True)
                pv_last = None
                group_tr = []
                for kb in range(NT):
                    sl = kb % 8
                    bnk = 6 + (kb // 8) % 2
                    ttr = transpose_op(p, banks, bnk,
                                       banks.bf16(bnk)[:, sl * 128:(sl + 1) * 128],
                                       PROW[:, kb * 128:(kb + 1) * 128], IDB,
                                       deps=[exp_toks[kb // 4], c_idb])
                    prow_readers.append(ttr)
                    group_tr.append(ttr)
                    if sl == 7:
                        def fev4(e, bnk=bnk):
                            return e.tensor_copy(PTS[:, 0:1024],
                                                 banks.bf16(bnk)[:, 0:1024])

                        tev = bank_read(p, banks, bnk, DVE, fev4,
                                        deps=_flat([group_tr, pts_readers]))
                        pts_readers = []
                        group_tr = []
                        for kb2 in range(kb - 7, kb + 1):
                            def fpv(e, kb2=kb2, h=h, s=(kb2 == 0),
                                    st_=(kb2 == NT - 1)):
                                return e.matmul(
                                    banks.f32(3)[:, 0:32],
                                    PTS[:, (kb2 % 8) * 128:(kb2 % 8 + 1) * 128],
                                    VTM[:, kb2 * 512 + h * 32:kb2 * 512 + h * 32 + 32],
                                    start=s, stop=st_, skip_group_check=True)

                            deps = _flat([tev, vtm_toks[kb2]]) + (
                                banks.war(3) if kb2 == 0 else [])
                            pv_last = p.op(PE, fpv, deps=deps)
                            pts_readers.append(pv_last)
                banks.wrote(3, pv_last)
                qkt_readers.append(pv_last)

                def fah(e, qi=qi, h=h):
                    return e.tensor_scalar(
                        AH[:, qi * 512 + h * 32:qi * 512 + h * 32 + 32],
                        banks.f32(3)[:, 0:32],
                        ST[:, SC_RSTI:SC_RSTI + 1], None, op0=ALU.mult)

                tah = bank_read(p, banks, 3, DVE, fah, deps=[tri_])
                ah_toks.append(tah)
    qkt_readers.extend(qk_rd)

    AHT = bf16(B0_R2 + 0, 4096)
    STGA2 = [bf16(B0_R2 + 4096, 256), bf16(B0_R2 + 4352, 256)]

    def produce_ah(i, stg_ap, pdeps):
        return p.op(DVE, (lambda i=i, stg_ap=stg_ap: (lambda e: e.tensor_copy(
            stg_ap, AH[:, i * 512:(i + 1) * 512])))(),
            deps=_flat([pdeps, ah_toks, qkt_readers]))

    aht_toks = emit_cast_transpose(
        p, banks, [None] * NT, AHT, STGA2, IDB, [[c_idb]] * NT,
        war_dst=qkt_readers, width=512, produce=produce_ah)
    WOTv = bf16(B0_R2 + 11776, 2048)
    wo_ev = {}
    wall = [wev[(w, r)] for w in range(3) for r in range(4)]
    for srow in range(8):
        b = srow % 2
        ldw = p.dma(POOL, STGW[b], Wo[srow * 128:(srow + 1) * 128, :],
                    deps=_flat([qkt_readers, wall, aht_toks, wo_ev.get(srow - 2)]))
        trs = [transpose_op(p, banks, 5, banks.bf16(5)[:, kc * 128:(kc + 1) * 128],
                            STGW[b][:, kc * 128:(kc + 1) * 128], IDB,
                            deps=[ldw, c_idb])
               for kc in range(4)]

        def fev5(e, srow=srow):
            dst = WOTv.rearrange("p (k o) -> p k o", k=4)[
                :, :, srow * 128:(srow + 1) * 128]
            srcv = banks.bf16(5)[:, 0:512].rearrange("p (k o) -> p k o", k=4)
            return e.tensor_copy(dst, srcv)

        wo_ev[srow] = bank_read(p, banks, 5, DVE, fev5, deps=trs)
    ldbo = p.dma(SP, SCR[0:1, 0:1024], bo[None, :], deps=_flat([qkt_readers]))
    bob_toks = []
    for half in range(2):
        war = banks.war(4)

        def fbm(e, half=half):
            return e.matmul(banks.f32(4)[:, 0:512], ONES32[0:1, 0:128],
                            SCR[0:1, half * 512:(half + 1) * 512],
                            start=True, stop=True, skip_group_check=True)

        tm = p.op(PE, fbm, deps=[ldbo, c_o32] + war)
        banks.wrote(4, tm)

        def fbe(e, half=half):
            return e.tensor_copy(BOBC[:, half * 512:(half + 1) * 512],
                                 banks.f32(4)[:, 0:512])

        bob_toks.append(bank_read(p, banks, 4, DVE, fbe))
    attn_add_toks = []
    wo_all = list(wo_ev.values())
    for ti in range(NT):
        for half in range(2):
            bank = (ti % 2) * 2 + half
            terms = [(AHT[:, kc * T + ti * 128:kc * T + ti * 128 + 128],
                      WOTv[:, kc * 1024 + half * 512:kc * 1024 + half * 512 + 512],
                      [aht_toks, wo_all]) for kc in range(4)]
            mt = mm_group(p, banks, bank, banks.f32(bank)[:, 0:512], terms)
            Xsl = X[:, ti * D + half * 512:ti * D + half * 512 + 512]

            def fx1(e, Xsl=Xsl, bank=bank):
                return e.tensor_tensor(Xsl, Xsl, banks.f32(bank)[:, 0:512], op=ALU.add)

            t1 = bank_read(p, banks, bank, DVE, fx1, deps=[mt, ld_x])

            def fx2(e, Xsl=Xsl, half=half):
                return e.tensor_tensor(Xsl, Xsl, BOBC[:, half * 512:(half + 1) * 512],
                                       op=ALU.add)

            t2 = p.op(DVE, fx2, deps=_flat([t1, bob_toks]))
            attn_add_toks.append(t2)

    if dbg is not None:
        td = p.dma(POOL, dbg["outs"]["AH"], AH.bitcast(F32), deps=_flat([ah_toks]))
        dbg.setdefault("dump_toks", []).append(td)
        attn_add_toks.append(td)
        td2 = p.dma(POOL, dbg["outs"]["X1"], X, deps=_flat([attn_add_toks]))
        dbg.setdefault("dump_toks", []).append(td2)
        attn_add_toks.append(td2)
    st["attn_toks"] = attn_add_toks
    st["xnt_toks"] = xntl_toks
    st["last_xnt_readers"] = list(qkt_readers)
    st["work_gate"] = _flat([qkt_readers, pts_readers, prow_readers, ah_toks,
                             aht_toks, attn_add_toks])
    st["xq_pre"] = (xqt_toks, _flat([xn_readers, attn_add_toks]))
    st["dbg"] = dbg
    return st


def build_block1(debug=False):
    nc = bass.Bass("TRN2", target_bir_lowering=False)
    x = nc.dram_tensor("x", [T, D], F32, kind="ExternalInput").ap()
    Wq = nc.dram_tensor("Wq", [512, 512], F32, kind="ExternalInput").ap()
    Wk = nc.dram_tensor("Wk", [512, 512], F32, kind="ExternalInput").ap()
    Wv = nc.dram_tensor("Wv", [512, 512], F32, kind="ExternalInput").ap()
    Wo = nc.dram_tensor("Wo", [D, 512], F32, kind="ExternalInput").ap()
    bo = nc.dram_tensor("bo", [D], F32, kind="ExternalInput").ap()
    wg = nc.dram_tensor("w_gate", [4 * D, D], F32, kind="ExternalInput").ap()
    wu = nc.dram_tensor("w_up", [4 * D, D], F32, kind="ExternalInput").ap()
    wd = nc.dram_tensor("w_down", [D, 4 * D], F32, kind="ExternalInput").ap()
    y = nc.dram_tensor("y", [T, D], F32, kind="ExternalOutput").ap()
    hdram = nc.dram_tensor("hdram", [T, 4 * D], BF16, kind="Internal").ap()
    w2t = nc.dram_tensor("w2t", [4 * D, D], BF16, kind="Internal").ap()
    env = std_env(nc, 47104)
    p = Prog(nc)
    dbg = None
    if debug:
        shapes = {"QTB": [128, 1024], "KTB": [128, 1024], "VTMB": [128, 4096],
                  "XNTL": [128, 4096], "BND": [128, 320], "AH": [128, 4096],
                  "X1": [128, NT * D]}
        outs = {nm: nc.dram_tensor("d_" + nm, sh, F32, kind="ExternalOutput").ap()
                for nm, sh in shapes.items()}
        dbg = {"outs": outs}
    st = emit_block1(p, env, x, Wq, Wk, Wv, Wo, bo, y, dbg=dbg)
    emit_bitlinear_ffn(p, env, st, [wg, wu], wd, y, hdram, w2t, mode="swiglu")
    p.emit(env["sems"])
    return nc



P2 = {"er": 180, "phi": 184, "aab": 188, "are": 190, "cre": 194, "cim": 196,
      "t0": 202, "t1": 204, "t2": 206, "t3": 208, "t4": 210}


def emit_block2_s5(p, env, st, prm):
    AR, ST, SCR = env["AR"], env["ST"], env["SCR"]
    ONES32, IDB = env["ONES32"], env["IDB"]
    banks = env["banks"]
    f32, bf16, i32 = arena_views(AR)
    X, XN, XNT = st["X"], st["XN"], st["XNT"]
    xnt_toks, ld_x = st["xnt_toks"], st["ld_x"]
    c_idb = st["consts"][0]
    BCT = bf16(B0_WK + 0, 4096)
    COSB = f32(B0_WK + 4096, 512)
    SINB = f32(B0_WK + 4608, 512)
    DDIAG = bf16(B0_WK + 5120, 512)
    STGB = bf16(B0_WK + 5632, 512)

    def bu(d, reim, pt):
        return f32(B0_R2 + d * 8192 + reim * 4096 + pt * 2048, 2048)

    def pcol(nm):
        return ST[:, P2[nm]:P2[nm] + 2]

    def pc1(nm, pt):
        return ST[:, P2[nm] + pt:P2[nm] + pt + 1]

    scan_prev = []
    cproj_done = {0: [], 1: []}
    scr_chain = list(st["xnt_toks"])
    for d in range(2):
        P = prm[d]
        oth = 1 - d
        VRE = f32(B0_R2 + oth * 8192 + 0, 2048)
        VIM = f32(B0_R2 + oth * 8192 + 2048, 2048)
        DEC = f32(B0_R2 + oth * 8192 + 4096, 2048)
        XSB = {(0, 0): bf16(B0_R2 + oth * 8192 + 6144, 1024),
               (1, 0): bf16(B0_R2 + oth * 8192 + 7168, 1024),
               (0, 1): bf16(B0_R2 + oth * 8192 + 4096, 1024),
               (1, 1): bf16(B0_R2 + oth * 8192 + 5120, 1024)}
        LR, IM, LD = pcol("t0"), pcol("t1"), pcol("t2")
        pgate = list(cproj_done[oth])
        l1 = [p.dma(SP, LR[:, pt:pt + 1], P["log_real"][pt * 128:(pt + 1) * 128, None],
                    deps=pgate)
              for pt in range(2)]
        l2 = [p.dma(SP, IM[:, pt:pt + 1], P["imag"][pt * 128:(pt + 1) * 128, None],
                    deps=pgate)
              for pt in range(2)]
        l3 = [p.dma(SP, LD[:, pt:pt + 1], P["log_Delta"][pt * 128:(pt + 1) * 128, None],
                    deps=pgate)
              for pt in range(2)]
        ER, PHI = pcol("er"), pcol("phi")
        AAB, ARE, AIM = pcol("aab"), pcol("are"), ST[:, 192:194]
        CRE, CIM = pcol("cre"), pcol("cim")
        DLT, NEGR = ST[:, 182:184], ST[:, 186:188]
        TT3, TT4 = pcol("t3"), pcol("t4")
        t_er = p.op(ACT, lambda e: e.activation(ER, LR, AF.Exp), deps=l1, async_=True)
        t_dl = p.op(ACT, lambda e: e.activation(DLT, LD, AF.Exp), deps=l3, async_=True)
        t_r = p.op(DVE, lambda e: e.scalar_tensor_tensor(
            NEGR, ER, -1.0, DLT, op0=ALU.mult, op1=ALU.mult), deps=[t_er, t_dl],
            async_=True)
        t_ph = p.op(DVE, lambda e: e.tensor_tensor(PHI, IM, DLT, op=ALU.mult),
                    deps=_flat([l2, t_dl]), async_=True)
        t_ab = p.op(ACT, lambda e: e.activation(AAB, NEGR, AF.Exp), deps=[t_r],
                    async_=True)

        def rrsin(dst, src_ap, add_half_pi, deps):
            dd = list(deps)
            K1, K2, K3 = ST[:, 212:214], ST[:, 214:216], ST[:, 216:218]
            if add_half_pi:
                ta = p.op(DVE, (lambda src_ap=src_ap: (lambda e: e.tensor_scalar(
                    K3, src_ap, 1.5707963267948966, None, op0=ALU.add)))(), deps=dd,
                    async_=True)
                srcv, dd = K3, [ta]
            else:
                srcv = src_ap
            tk1 = p.op(DVE, (lambda srcv=srcv: (lambda e: e.tensor_scalar(
                K1.bitcast(I32), srcv, 0.15915494309189535, None,
                op0=ALU.mult)))(), deps=dd, async_=True)
            tk2 = p.op(DVE, lambda e: e.tensor_copy(K2, K1.bitcast(I32)),
                       deps=[tk1], async_=True)
            tk3 = p.op(DVE, (lambda srcv=srcv: (lambda e: e.scalar_tensor_tensor(
                K1, K2, -6.28125, srcv, op0=ALU.mult, op1=ALU.add)))(),
                deps=[tk2] + dd, async_=True)
            tk4 = p.op(DVE, lambda e: e.scalar_tensor_tensor(
                K1, K2, -0.0019353071795864769, K1, op0=ALU.mult, op1=ALU.add),
                deps=[tk3], async_=True)
            tk5 = p.op(DVE, lambda e: e.tensor_scalar(
                K1, K1, 3.14159265, -3.14159265, op0=ALU.min, op1=ALU.max),
                deps=[tk4], async_=True)
            return p.op(ACT, (lambda dst=dst: (lambda e: e.activation(
                dst, K1, AF.Sin)))(), deps=[tk5], async_=True)

        t_cos = rrsin(TT3, PHI, True, [t_ph])
        t_are = p.op(DVE, lambda e: e.tensor_tensor(ARE, AAB, TT3, op=ALU.mult),
                     deps=[t_ab, t_cos], async_=True)
        t_sin = rrsin(TT3, PHI, False, [t_ph, t_are])
        t_aim = p.op(DVE, lambda e: e.tensor_tensor(AIM, AAB, TT3, op=ALU.mult),
                     deps=[t_ab, t_sin], async_=True)
        MSK, LSR, LSI = ST[:, 218:220], ST[:, 220:222], ST[:, 222:224]
        CST, WS, WI = ST[:, 224:226], ST[:, 226:228], ST[:, 228:230]
        NRE = ST[:, 230:232]
        t_w = p.op(DVE, lambda e: e.scalar_tensor_tensor(
            WS, ER, 1.0, ER, op0=ALU.mult, op1=ALU.mult), deps=[t_er], async_=True)
        t_w2 = p.op(DVE, lambda e: e.scalar_tensor_tensor(
            TT4, IM, 1.0, IM, op0=ALU.mult, op1=ALU.mult), deps=_flat([l2, t_aim]),
            async_=True)
        t_w3 = p.op(DVE, lambda e: e.tensor_tensor(WS, WS, TT4, op=ALU.add),
                    deps=[t_w, t_w2], async_=True)
        MSKI = MSK.bitcast(I32)
        t_m = p.op(DVE, lambda e: e.tensor_scalar(MSKI, WS, 1e-12, None,
                                                  op0=ALU.is_lt),
                   deps=[t_w3], async_=True)
        tm1 = p.op(DVE, lambda e: e.memset(CST, 1e-6), deps=[], async_=True)
        t_l1 = p.op(DVE, lambda e: e.tensor_scalar(LSR, ER, -1.0, None,
                                                   op0=ALU.mult),
                    deps=[t_er], async_=True)
        t_l2 = p.op(DVE, lambda e: e.select(LSR, MSKI, CST, LSR),
                    deps=[t_m, t_l1, tm1], async_=True)
        t_l3 = p.op(DVE, lambda e: e.memset(CST, 0.0), deps=[t_l2], async_=True)
        t_l4 = p.op(DVE, lambda e: e.select(LSI, MSKI, CST, IM),
                    deps=_flat([t_m, t_l3, l2]), async_=True)
        t_w4 = p.op(DVE, lambda e: e.scalar_tensor_tensor(
            WS, LSR, 1.0, LSR, op0=ALU.mult, op1=ALU.mult), deps=[t_l2, t_w3],
            async_=True)
        t_w5 = p.op(DVE, lambda e: e.scalar_tensor_tensor(
            TT4, LSI, 1.0, LSI, op0=ALU.mult, op1=ALU.mult), deps=[t_l4, t_w3],
            async_=True)
        t_w6 = p.op(DVE, lambda e: e.tensor_tensor(WS, WS, TT4, op=ALU.add),
                    deps=[t_w4, t_w5], async_=True)
        t_wi = p.op(DVE, lambda e: e.reciprocal(WI, WS), deps=[t_w6], async_=True)
        t_n1 = p.op(DVE, lambda e: e.tensor_scalar(NRE, ARE, -1.0, None,
                                                   op0=ALU.add),
                    deps=[t_are], async_=True)
        t_c1 = p.op(DVE, lambda e: e.tensor_tensor(TT3, NRE, LSR, op=ALU.mult),
                    deps=[t_n1, t_l2, t_aim], async_=True)
        t_c2 = p.op(DVE, lambda e: e.tensor_tensor(TT4, AIM, LSI, op=ALU.mult),
                    deps=[t_aim, t_l4, t_w6], async_=True)
        t_c3 = p.op(DVE, lambda e: e.tensor_tensor(TT3, TT3, TT4, op=ALU.add),
                    deps=[t_c1, t_c2], async_=True)
        t_c4 = p.op(DVE, lambda e: e.tensor_tensor(CRE, TT3, WI, op=ALU.mult),
                    deps=[t_c3, t_wi], async_=True)
        t_c5 = p.op(DVE, lambda e: e.tensor_tensor(TT3, AIM, LSR, op=ALU.mult),
                    deps=[t_aim, t_l2, t_c3], async_=True)
        t_c6 = p.op(DVE, lambda e: e.tensor_tensor(TT4, NRE, LSI, op=ALU.mult),
                    deps=[t_n1, t_l4, t_c3], async_=True)
        t_c7 = p.op(DVE, lambda e: e.tensor_tensor(TT3, TT3, TT4, op=ALU.subtract),
                    deps=[t_c5, t_c6], async_=True)
        t_c8 = p.op(DVE, lambda e: e.tensor_tensor(CIM, TT3, WI, op=ALU.mult),
                    deps=[t_c7, t_wi], async_=True)

        if (st.get("dbg") is not None and st.get("upto") == "d1par"
                and d == 1):
            dbg = st["dbg"]
            td = p.dma(POOL, dbg["outs"]["X1"], X,
                       deps=_flat([t_c4, t_c8, t_ab, t_ph, cproj_done[0]]))
            dbg.setdefault("dump_toks", []).append(td)
            p.op(POOL, lambda e: e.nop(), deps=_flat([dbg["dump_toks"]]))
            return [], []
        bb_toks = []
        for reim in range(2):
            for pt in range(2):
                lb1 = p.dma(SP, SCR[:, 0:1024],
                            P["B_real"][pt * 128:(pt + 1) * 128, :],
                            deps=_flat([scr_chain, scan_prev]))
                lb2 = p.dma(SP, SCR[:, 1024:2048],
                            P["B_imag"][pt * 128:(pt + 1) * 128, :],
                            deps=_flat([scr_chain, scan_prev]))
                if reim == 0:
                    ta = p.op(DVE, (lambda pt=pt: (lambda e: e.tensor_scalar(
                        SCR[:, 0:1024], SCR[:, 0:1024], pc1("cre", pt), None,
                        op0=ALU.mult)))(), deps=[lb1, t_c4], async_=True)
                    tb2 = p.op(DVE, (lambda pt=pt: (lambda e: e.tensor_scalar(
                        SCR[:, 1024:2048], SCR[:, 1024:2048], pc1("cim", pt), None,
                        op0=ALU.mult)))(), deps=[lb2, t_c8], async_=True)
                    tc = p.op(DVE, lambda e: e.tensor_tensor(
                        SCR[:, 0:1024], SCR[:, 0:1024], SCR[:, 1024:2048],
                        op=ALU.subtract), deps=[ta, tb2], async_=True)
                else:
                    ta = p.op(DVE, (lambda pt=pt: (lambda e: e.tensor_scalar(
                        SCR[:, 1024:2048], SCR[:, 1024:2048], pc1("cre", pt), None,
                        op0=ALU.mult)))(), deps=[lb2, t_c4], async_=True)
                    tb2 = p.op(DVE, (lambda pt=pt: (lambda e: e.tensor_scalar(
                        SCR[:, 0:1024], SCR[:, 0:1024], pc1("cim", pt), None,
                        op0=ALU.mult)))(), deps=[lb1, t_c8], async_=True)
                    tc = p.op(DVE, lambda e: e.tensor_tensor(
                        SCR[:, 0:1024], SCR[:, 1024:2048], SCR[:, 0:1024],
                        op=ALU.add), deps=[ta, tb2], async_=True)
                tcast = p.op(DVE, lambda e: e.tensor_copy(STGB[:], SCR[:, 0:1024]),
                             deps=_flat([tc, scan_prev]), async_=True)
                scr_chain = [tcast]
                trs = [transpose_op(p, banks, 4,
                                    banks.bf16(4)[:, kk * 128:(kk + 1) * 128],
                                    STGB[:, kk * 128:(kk + 1) * 128], IDB,
                                    deps=[tcast, c_idb]) for kk in range(8)]

                def fbev(e, reim=reim, pt=pt):
                    dst = BCT.rearrange("p (k c) -> p k c", k=16)[
                        :, reim * 8:(reim + 1) * 8, pt * 128:(pt + 1) * 128]
                    srcv = banks.bf16(4)[:, 0:1024].rearrange("p (k c) -> p k c",
                                                              k=8)
                    return e.tensor_copy(dst, srcv)

                bb_toks.append(bank_read(p, banks, 4, DVE, fbev,
                                         deps=_flat([trs, scan_prev])))
        if st.get("dbg") is not None and d == 0:
            dbg = st["dbg"]
            tdb = p.dma(POOL, dbg["outs"]["BCT"], BCT.bitcast(F32), deps=bb_toks)
            dbg.setdefault("dump_toks", []).append(tdb)
            bb_toks = list(bb_toks) + [tdb]
        scr_chain = [bb_toks[-1]]
        if (st.get("dbg") is not None and st.get("upto") == "d1bb"
                and d == 1):
            dbg = st["dbg"]
            td = p.dma(POOL, dbg["outs"]["X1"], X,
                       deps=_flat([bb_toks, cproj_done[0]]))
            dbg.setdefault("dump_toks", []).append(td)
            p.op(POOL, lambda e: e.nop(), deps=_flat([dbg["dump_toks"]]))
            return [], []

        bu_toks = {}
        for reim in range(2):
            for pt in range(2):
                evs = []
                for tc2 in range(4):
                    bank = tc2
                    terms = [(BCT[:, reim * 4096 + kc * 512 + pt * 128:
                                  reim * 4096 + kc * 512 + pt * 128 + 128],
                              XNT[:, kc * T + tc2 * 512:kc * T + tc2 * 512 + 512],
                              [bb_toks, xnt_toks]) for kc in range(8)]
                    mt = mm_group(p, banks, bank, banks.f32(bank)[:, 0:512], terms,
                                  extra=scan_prev)

                    def fbuev(e, d=d, reim=reim, pt=pt, tc2=tc2, bank=bank):
                        return e.tensor_copy(
                            bu(d, reim, pt)[:, tc2 * 512:(tc2 + 1) * 512],
                            banks.f32(bank)[:, 0:512])

                    evs.append(bank_read(p, banks, bank, DVE, fbuev,
                                         deps=scan_prev))
                bu_toks[(reim, pt)] = evs
        scan_prev = []
        if (st.get("dbg") is not None and st.get("upto") == "d1bu"
                and d == 1):
            dbg = st["dbg"]
            td = p.dma(POOL, dbg["outs"]["X1"], X,
                       deps=_flat([list(bu_toks.values()), cproj_done[0]]))
            dbg.setdefault("dump_toks", []).append(td)
            p.op(POOL, lambda e: e.nop(), deps=_flat([dbg["dump_toks"]]))
            return [], []

        if st.get("dbg") is not None and d == 0:
            dbg = st["dbg"]
            for nm, ap, dp in (("BU00", bu(0, 0, 0), bu_toks[(0, 0)]),
                               ("STP", ST[:, 0:320], [t_c4, t_c8, t_ab, t_ph])):
                td = p.dma(POOL, dbg["outs"][nm], ap, deps=_flat([dp]))
                dbg.setdefault("dump_toks", []).append(td)
                for kk in bu_toks:
                    bu_toks[kk] = list(bu_toks[kk]) + [td]
            if st.get("upto") == "bu":
                p.op(POOL, lambda e: e.nop(), deps=_flat([dbg["dump_toks"]]))
                return [], []
        xsb_toks = []
        s_prev = {}
        trig_rd = []
        for pt in range(2):
            t_dec = p.op(DVE, (lambda d=d, pt=pt, DEC=DEC: (lambda e: e.tensor_scalar(
                DEC, bu(d, 0, pt), 0.0, pc1("aab", pt), op0=ALU.mult,
                op1=ALU.add)))(),
                deps=_flat([t_ab, bu_toks[(0, pt)], cproj_done[oth]]), async_=True)
            for half in range(4):
                t_rmp = p.op(POOL, lambda e: e.iota(
                    SCR[:, 0:512], pattern=[[1, 512]], base=0,
                    channel_multiplier=0, allow_small_or_imprecise_dtypes=True),
                    deps=_flat([scr_chain]))
                scr_chain = [t_rmp]
                t_a1 = p.op(DVE, (lambda pt=pt, half=half: (lambda e:
                            e.tensor_scalar(
                                SCR[:, 512:1024], SCR[:, 0:512],
                                float(half * 512), pc1("phi", pt),
                                op0=ALU.add, op1=ALU.mult)))(),
                    deps=_flat([t_rmp, t_ph, bu_toks[(1, 1)], trig_rd]),
                    async_=True)
                trig_rd = []
                t_a2 = p.op(DVE, lambda e: e.tensor_scalar(
                    SCR[:, 0:512].bitcast(I32), SCR[:, 512:1024],
                    0.15915494309189535, None, op0=ALU.mult), deps=[t_a1],
                    async_=True)
                t_a3 = p.op(DVE, lambda e: e.tensor_copy(
                    COSB, SCR[:, 0:512].bitcast(I32)), deps=[t_a2], async_=True)
                t_a4 = p.op(DVE, lambda e: e.scalar_tensor_tensor(
                    SCR[:, 512:1024], COSB, -6.28125, SCR[:, 512:1024],
                    op0=ALU.mult, op1=ALU.add), deps=[t_a3], async_=True)
                t_a5 = p.op(DVE, lambda e: e.scalar_tensor_tensor(
                    SCR[:, 512:1024], COSB, -0.0019353071795864769,
                    SCR[:, 512:1024], op0=ALU.mult, op1=ALU.add), deps=[t_a4],
                    async_=True)
                t_a6 = p.op(DVE, lambda e: e.tensor_scalar(
                    SCR[:, 512:1024], SCR[:, 512:1024], 3.14159265, -3.14159265,
                    op0=ALU.min, op1=ALU.max), deps=[t_a5], async_=True)
                t_sin2 = p.op(ACT, lambda e: e.activation(SINB, SCR[:, 512:1024],
                                                          AF.Sin), deps=[t_a6],
                              async_=True)
                t_b1 = p.op(DVE, lambda e: e.tensor_scalar(
                    COSB, SCR[:, 512:1024], 1.5707963267948966, None,
                    op0=ALU.add), deps=[t_a6, t_sin2, t_a3], async_=True)
                t_b2 = p.op(DVE, lambda e: e.tensor_scalar(
                    SCR[:, 0:512], COSB, 3.14159265358, None, op0=ALU.is_gt),
                    deps=[t_b1, t_a2], async_=True)
                t_b3 = p.op(DVE, lambda e: e.scalar_tensor_tensor(
                    COSB, SCR[:, 0:512], -6.283185307179586, COSB,
                    op0=ALU.mult, op1=ALU.add), deps=[t_b2], async_=True)
                t_b4 = p.op(DVE, lambda e: e.tensor_scalar(
                    COSB, COSB, 3.14159265, -3.14159265, op0=ALU.min, op1=ALU.max),
                    deps=[t_b3], async_=True)
                t_cos2 = p.op(ACT, lambda e: e.activation(COSB, COSB, AF.Sin),
                              deps=[t_b4], async_=True)
                sl = slice(half * 512, (half + 1) * 512)
                if d == 0:
                    osl = sl
                    bre = bu(d, 0, pt)[:, sl]
                    bim = bu(d, 1, pt)[:, sl]
                else:
                    osl = slice((3 - half) * 512, (4 - half) * 512)
                    lo = 2047 - (half + 1) * 512
                    rs = slice(2047 - half * 512, None if lo < 0 else lo, -1)
                    bre = bu(d, 0, pt)[:, rs]
                    bim = bu(d, 1, pt)[:, rs]
                t_v1 = p.op(DVE, (lambda bre=bre: (lambda e: e.tensor_tensor(
                    SCR[:, 512:1024], bre, COSB, op=ALU.mult)))(),
                    deps=_flat([t_cos2, bu_toks[(0, pt)], cproj_done[oth],
                                scr_chain]), async_=True)
                t_v2 = p.op(DVE, (lambda bim=bim: (lambda e: e.tensor_tensor(
                    SCR[:, 0:512], bim, SINB, op=ALU.mult)))(),
                    deps=_flat([t_sin2, bu_toks[(1, pt)], t_v1]), async_=True)
                t_v3 = p.op(DVE, (lambda VRE=VRE, sl=sl: (lambda e:
                            e.tensor_tensor(VRE[:, sl], SCR[:, 512:1024],
                                            SCR[:, 0:512], op=ALU.add)))(),
                    deps=[t_v1, t_v2], async_=True)
                t_v4 = p.op(DVE, (lambda bim=bim: (lambda e: e.tensor_tensor(
                    SCR[:, 512:1024], bim, COSB, op=ALU.mult)))(),
                    deps=_flat([t_cos2, bu_toks[(1, pt)], t_v3]), async_=True)
                t_v5 = p.op(DVE, (lambda bre=bre: (lambda e: e.tensor_tensor(
                    SCR[:, 0:512], bre, SINB, op=ALU.mult)))(),
                    deps=_flat([t_sin2, bu_toks[(0, pt)], t_v4]), async_=True)
                t_v6 = p.op(DVE, (lambda VIM=VIM, sl=sl: (lambda e:
                            e.tensor_tensor(VIM[:, sl], SCR[:, 512:1024],
                                            SCR[:, 0:512], op=ALU.subtract)))(),
                    deps=[t_v4, t_v5], async_=True)
                scr_chain = [t_v6, t_v5]
                if (st.get("dbg") is not None and st.get("upto") == "rot"
                        and d == 0 and pt == 0 and half == 3):
                    dbg = st["dbg"]
                    for nm, ap in (("VRE0", VRE), ("VIM0", VIM), ("DEC0", DEC),
                                   ("COSL", COSB), ("SINL", SINB)):
                        td = p.dma(POOL, dbg["outs"][nm], ap,
                                   deps=[t_v3, t_v6, t_dec, t_cos2, t_sin2])
                        dbg.setdefault("dump_toks", []).append(td)
                    p.op(POOL, lambda e: e.nop(),
                         deps=_flat([dbg["dump_toks"]]))
                    return [], []
                if half == 0:
                    init_re, init_im = 0.0, 0.0
                    sdep = []
                else:
                    pv = half * 512 - 1 if d == 0 else (4 - half) * 512 + 511
                    init_re = bu(d, 0, pt)[:, pv:pv + 1]
                    init_im = bu(d, 1, pt)[:, pv:pv + 1]
                    sdep = [s_prev[(0, pt)], s_prev[(1, pt)]]
                t_s1 = p.op(DVE, (lambda d=d, pt=pt, sl=sl, osl=osl,
                            init_re=init_re, DEC=DEC, VRE=VRE:
                            (lambda e: e.tensor_tensor_scan(
                                bu(d, 0, pt)[:, osl], DEC[:, sl], VRE[:, sl],
                                init_re, op0=ALU.mult, op1=ALU.add)))(),
                    deps=_flat([t_dec, t_v3, sdep]), async_=True)
                t_s2 = p.op(DVE, (lambda d=d, pt=pt, sl=sl, osl=osl,
                            init_im=init_im, DEC=DEC, VIM=VIM:
                            (lambda e: e.tensor_tensor_scan(
                                bu(d, 1, pt)[:, osl], DEC[:, sl], VIM[:, sl],
                                init_im, op0=ALU.mult, op1=ALU.add)))(),
                    deps=_flat([t_dec, t_v6, sdep]), async_=True)
                s_prev[(0, pt)], s_prev[(1, pt)] = t_s1, t_s2
                t_u1 = p.op(DVE, (lambda d=d, pt=pt, osl=osl: (lambda e: e.tensor_tensor(
                    SCR[:, 512:1024], bu(d, 0, pt)[:, osl], COSB, op=ALU.mult)))(),
                    deps=[t_s1, t_cos2, t_v6], async_=True)
                t_u2 = p.op(DVE, (lambda d=d, pt=pt, osl=osl: (lambda e: e.tensor_tensor(
                    SCR[:, 0:512], bu(d, 1, pt)[:, osl], SINB, op=ALU.mult)))(),
                    deps=[t_s2, t_sin2, t_u1], async_=True)
                t_u3 = p.op(DVE, (lambda VRE=VRE, sl=sl: (lambda e:
                            e.tensor_tensor(VRE[:, sl], SCR[:, 512:1024],
                                            SCR[:, 0:512], op=ALU.subtract)))(),
                    deps=[t_u1, t_u2], async_=True)
                t_u4 = p.op(DVE, (lambda d=d, pt=pt, osl=osl: (lambda e: e.tensor_tensor(
                    SCR[:, 512:1024], bu(d, 1, pt)[:, osl], COSB, op=ALU.mult)))(),
                    deps=[t_s2, t_cos2, t_u3], async_=True)
                t_u5 = p.op(DVE, (lambda d=d, pt=pt, osl=osl: (lambda e: e.tensor_tensor(
                    SCR[:, 0:512], bu(d, 0, pt)[:, osl], SINB, op=ALU.mult)))(),
                    deps=[t_s1, t_sin2, t_u4], async_=True)
                t_u6 = p.op(DVE, (lambda VIM=VIM, sl=sl: (lambda e:
                            e.tensor_tensor(VIM[:, sl], SCR[:, 512:1024],
                                            SCR[:, 0:512], op=ALU.add)))(),
                    deps=[t_u4, t_u5], async_=True)
                scr_chain = [t_u5, t_u6]
                scan_prev.extend([t_u3, t_u6])
                trig_rd = [t_v1, t_v2, t_v4, t_v5, t_u1, t_u2, t_u4, t_u5]
            if (st.get("dbg") is not None and st.get("upto") == "scan"
                    and d == 0 and pt == 0):
                dbg = st["dbg"]
                for nm, ap in (("VRE0", VRE), ("VIM0", VIM), ("DEC0", DEC),
                               ("BU00", bu(0, 0, 0)), ("BU10", bu(0, 1, 0))):
                    td = p.dma(POOL, dbg["outs"][nm], ap,
                               deps=_flat([scan_prev, t_dec, s_prev[(0, 0)],
                                           s_prev[(1, 0)]]))
                    dbg.setdefault("dump_toks", []).append(td)
                p.op(POOL, lambda e: e.nop(), deps=_flat([dbg["dump_toks"]]))
                return [], []
            for reim, SRC in ((0, VRE), (1, VIM)):
                DSTB = XSB[(reim, pt)]
                if d == 0:
                    t_cx = p.op(DVE, (lambda SRC=SRC, DSTB=DSTB: (lambda e:
                                e.tensor_copy(DSTB[:, 0:2048], SRC)))(),
                                deps=_flat([scan_prev, t_dec]), async_=True)
                else:
                    t_cx = p.op(DVE, (lambda SRC=SRC, DSTB=DSTB: (lambda e:
                                e.tensor_copy(DSTB[:, 0:2048], SRC[:, ::-1])))(),
                                deps=_flat([scan_prev, t_dec]), async_=True)
                xsb_toks.append(t_cx)
                scan_prev.append(t_cx)
            if (st.get("dbg") is not None and st.get("upto") == "cast"
                    and d == 0 and pt == 0):
                dbg = st["dbg"]
                for nm, ap in (("XSB0", XSB[(0, 0)].bitcast(F32)),
                               ("XSB1", XSB[(1, 0)].bitcast(F32))):
                    td = p.dma(POOL, dbg["outs"][nm], ap,
                               deps=_flat([xsb_toks]))
                    dbg.setdefault("dump_toks", []).append(td)
                p.op(POOL, lambda e: e.nop(), deps=_flat([dbg["dump_toks"]]))
                return [], []

        if (st.get("dbg") is not None and st.get("upto") == "cast2"
                and d == 0):
            dbg = st["dbg"]
            for nm, ap in (("XSB0", XSB[(0, 0)].bitcast(F32)),
                           ("XSB1", XSB[(1, 0)].bitcast(F32))):
                td = p.dma(POOL, dbg["outs"][nm], ap, deps=_flat([xsb_toks]))
                dbg.setdefault("dump_toks", []).append(td)
            p.op(POOL, lambda e: e.nop(), deps=_flat([dbg["dump_toks"]]))
            return [], []
        dbg_gate = []
        if st.get("dbg") is not None and d == 0:
            dbg = st["dbg"]
            td = p.dma(POOL, dbg["outs"]["XSB0"], XSB[(0, 0)].bitcast(F32),
                       deps=_flat([xsb_toks]))
            dbg.setdefault("dump_toks", []).append(td)
            dbg_gate.append(td)
        ct_toks = []
        for reim in range(2):
            SRCW = (P["C_real"], P["C_imag"])[reim]
            for hs in range(8):
                ldc = p.dma(SP, SCR[:, 0:256], SRCW[hs * 128:(hs + 1) * 128, :],
                            deps=_flat([scr_chain, scan_prev]))
                tcst = p.op(DVE, (lambda reim=reim: (lambda e: e.tensor_scalar(
                    STGB[:, 0:256], SCR[:, 0:256],
                    -1.0 if reim == 1 else 1.0, None, op0=ALU.mult)))(),
                    deps=[ldc], async_=True)
                scr_chain = [tcst]
                trs = [transpose_op(p, banks, 4,
                                    banks.bf16(4)[:, kk * 128:(kk + 1) * 128],
                                    STGB[:, kk * 128:(kk + 1) * 128], IDB,
                                    deps=[tcst, c_idb]) for kk in range(2)]

                def fcev(e, reim=reim, hs=hs):
                    dst = BCT.rearrange("p (t c) -> p t c", t=8)[
                        :, reim * 2:reim * 2 + 2, hs * 128:(hs + 1) * 128]
                    srcv = banks.bf16(4)[:, 0:256].rearrange("p (t c) -> p t c",
                                                             t=2)
                    return e.tensor_copy(dst, srcv)

                ct_toks.append(bank_read(p, banks, 4, DVE, fcev,
                                         deps=_flat([trs, bu_toks[(1, 1)]])))
        DCOL = ST[:, 234:242]
        ld_d = [p.dma(SP, DCOL[:, c:c + 1], P["D"][c * 128:(c + 1) * 128, None])
                for c in range(8)]
        dd_toks = []
        for hc in range(8):
            def fdd(e, hc=hc):
                return e.tensor_scalar(DDIAG[:, hc * 128:(hc + 1) * 128], IDB[:],
                                       DCOL[:, hc:hc + 1], None, op0=ALU.mult)

            dd_toks.append(p.op(DVE, fdd,
                                deps=_flat([ld_d, c_idb, cproj_done[oth]]),
                                async_=True))
        cproj = []
        for ti in range(NT):
            for half in range(2):
                bank = (ti % 2) * 2 + half
                terms = []
                for reim in range(2):
                    for pt in range(2):
                        terms.append((
                            XSB[(reim, pt)][:, ti * 128:(ti + 1) * 128],
                            BCT.rearrange("p (t c) -> p t c", t=8)[
                                :, reim * 2 + pt, half * 512:(half + 1) * 512],
                            [xsb_toks, ct_toks]))
                war0 = banks.war(bank)
                mtoks = []
                for i2, (l_, r_, dl_) in enumerate(terms):
                    def fmm0(e, l_=l_, r_=r_, s=(i2 == 0), bank=bank):
                        return e.matmul(banks.f32(bank)[:, 0:512], l_, r_,
                                        start=s, stop=False, skip_group_check=True)
                    mtoks.append(p.op(PE, fmm0,
                                      deps=_flat([dl_]) + (war0 if i2 == 0 else [])))
                for j in range(4):
                    hc = half * 4 + j

                    def fmmd(e, hc=hc, j=j, ti=ti, bank=bank, st_=(j == 3)):
                        return e.matmul(
                            banks.f32(bank)[:, j * 128:(j + 1) * 128],
                            XNT[:, hc * T + ti * 128:hc * T + ti * 128 + 128],
                            DDIAG[:, hc * 128:(hc + 1) * 128],
                            start=False, stop=st_, skip_group_check=True)

                    mtoks.append(p.op(PE, fmmd, deps=_flat([dd_toks, xnt_toks])))
                mt = mtoks[-1]
                banks.wrote(bank, mt)

                def fgel(e, half=half, bank=bank):
                    return e.activation(SCR[:, half * 512:half * 512 + 512],
                                        banks.f32(bank)[:, 0:512], AF.Gelu)

                tg = bank_read(p, banks, bank, ACT, fgel,
                               deps=_flat([mt, scr_chain]), async_=True)
                scr_chain = []

                def fadd(e, ti=ti, half=half):
                    Xsl = X[:, ti * D + half * 512:ti * D + half * 512 + 512]
                    return e.tensor_tensor(Xsl, Xsl,
                                           SCR[:, half * 512:half * 512 + 512],
                                           op=ALU.add)

                ta2 = p.op(DVE, fadd, deps=[tg, ld_x], async_=True)
                cproj.append(ta2)
                scr_chain.append(ta2)
        if st.get("dbg") is not None:
            dbg = st["dbg"]
            td = p.dma(POOL, dbg["outs"]["X1" if d == 1 else "X0"], X,
                       deps=_flat([cproj]))
            dbg.setdefault("dump_toks", []).append(td)
            cproj.append(td)
        cproj_done[d] = _flat([cproj, ct_toks, dd_toks, dbg_gate])
        scan_prev = list(cproj_done[d])
        if (st.get("dbg") is not None and st.get("upto") == "cproj0"
                and d == 0):
            dbg = st["dbg"]
            td = p.dma(POOL, dbg["outs"]["XSB1"], XSB[(1, 0)].bitcast(F32),
                       deps=_flat([cproj_done[0]]))
            dbg.setdefault("dump_toks", []).append(td)
            p.op(POOL, lambda e: e.nop(), deps=_flat([dbg["dump_toks"]]))
            return [], []
    return _flat([cproj_done[0], cproj_done[1]]), scan_prev


def emit_block2_moe(p, env, st, wg, wfc, wpr, y, s5_toks):
    AR, ST, SCR = env["AR"], env["ST"], env["SCR"]
    ONES32, IDB = env["ONES32"], env["IDB"]
    banks = env["banks"]
    f32, bf16, i32 = arena_views(AR)
    X, XNT = st["X"], st["XNT"]
    xnt_toks = st["xnt_toks"]
    c_idb, c_o32 = st["consts"][0], st["consts"][3]
    WFCT = bf16(B0_R2 + 0, 2048)
    WPT = bf16(B0_R2 + 2048, 2048)
    HM = bf16(B0_R2 + 4096, 4096)
    GT = f32(B0_R2 + 8192, 128)
    WG8B = bf16(B0_R2 + 8320, 512)
    WGT = bf16(B0_R2 + 8832, 32)
    STGM = [bf16(B0_R2 + 8864, 512), bf16(B0_R2 + 9376, 512)]

    ldg = p.dma(POOL, WG8B[0:8, :], wg[:, :], deps=_flat([s5_toks]))
    wgt_toks = []
    for kc in range(8):
        tr = transpose_op(p, banks, 4, banks.bf16(4)[:, 0:8],
                          WG8B[0:8, kc * 128:(kc + 1) * 128],
                          IDB[0:8, 0:8], deps=[ldg, c_idb])

        def fgev(e, kc=kc):
            return e.tensor_copy(WGT[:, kc * 8:(kc + 1) * 8], banks.bf16(4)[:, 0:8])

        wgt_toks.append(bank_read(p, banks, 4, DVE, fgev, deps=[tr]))
    gate_toks = []
    for ti in range(NT):
        bank = 4 + ti % 2
        terms = [(XNT[:, kc * T + ti * 128:kc * T + ti * 128 + 128],
                  WGT[:, kc * 8:(kc + 1) * 8], [wgt_toks, xnt_toks])
                 for kc in range(8)]
        mt = mm_group(p, banks, bank, banks.f32(bank)[:, 0:8], terms,
                      extra=s5_toks)
        M8 = ST[:, 242:250]

        def fm8(e, bank=bank):
            return e.max(M8, banks.f32(bank)[:, 0:8])

        t1 = bank_read(p, banks, bank, DVE, fm8, deps=[mt], async_=True)
        t2 = p.op(DVE, lambda e: e.tensor_tensor(
            ST[:, 250:251], M8[:, 1:2], M8[:, 0:1], op=ALU.subtract), deps=[t1],
            async_=True)
        t3 = p.op(ACT, lambda e: e.activation(ST[:, 251:252], ST[:, 250:251],
                                              AF.Exp), deps=[t2], async_=True)
        t4 = p.op(DVE, lambda e: e.tensor_scalar(ST[:, 251:252], ST[:, 251:252],
                                                 1.0, None, op0=ALU.add),
                  deps=[t3], async_=True)
        t5 = p.op(DVE, lambda e: e.reciprocal(ST[:, 252:253], ST[:, 251:252]),
                  deps=[t4], async_=True)
        NEGM1 = ST[:, 253:254]
        t6 = p.op(DVE, lambda e: e.tensor_scalar(NEGM1, M8[:, 0:1], -1.0, None,
                                                 op0=ALU.mult), deps=[t1],
                  async_=True)

        def fE(e, bank=bank):
            return e.activation(SCR[:, 0:8], banks.f32(bank)[:, 0:8], AF.Exp,
                                bias=NEGM1)

        t7 = bank_read(p, banks, bank, ACT, fE, deps=[t6], async_=True)

        def fmask(e, bank=bank):
            return e.tensor_scalar(SCR[:, 8:16], banks.f32(bank)[:, 0:8],
                                   M8[:, 1:2], None, op0=ALU.is_ge)

        t8 = bank_read(p, banks, bank, DVE, fmask, deps=[t1, t7], async_=True)

        def fG(e, ti=ti):
            return e.scalar_tensor_tensor(GT[:, ti * 8:(ti + 1) * 8], SCR[:, 0:8],
                                          ST[:, 252:253], SCR[:, 8:16],
                                          op0=ALU.mult, op1=ALU.mult)

        gate_toks.append(p.op(DVE, fG, deps=[t5, t7, t8], async_=True))
    if st.get("dbg") is not None:
        dbg = st["dbg"]
        td = p.dma(POOL, dbg["outs"]["GT"], GT, deps=_flat([gate_toks]))
        dbg.setdefault("dump_toks", []).append(td)
        gate_toks.append(td)

    moe_adds = []
    wf_rd = []
    wp_rd = []
    hm_rd = []
    for ex in range(8):
        wf_toks = []
        for srow in range(4):
            b = srow % 2
            ldw = p.dma(POOL, STGM[b], wfc[ex, srow * 128:(srow + 1) * 128, :],
                        deps=_flat([wf_rd if srow < 2 else [],
                                    wf_toks[-2:] if srow >= 2 else [], s5_toks]))
            trs = [transpose_op(p, banks, 4,
                                banks.bf16(4)[:, kk * 128:(kk + 1) * 128],
                                STGM[b][:, kk * 128:(kk + 1) * 128], IDB,
                                deps=[ldw, c_idb]) for kk in range(8)]

            def fwev(e, srow=srow):
                dst = WFCT.rearrange("p (k c) -> p k c", k=8)[
                    :, :, srow * 128:(srow + 1) * 128]
                srcv = banks.bf16(4)[:, 0:1024].rearrange("p (k c) -> p k c", k=8)
                return e.tensor_copy(dst, srcv)

            wf_toks.append(bank_read(p, banks, 4, DVE, fwev,
                                     deps=_flat([trs, wf_rd])))
        wf_rd = []
        hm_toks = []
        for ic in range(4):
            for tc in range(4):
                bank = tc
                terms = [(WFCT[:, kc * 512 + ic * 128:kc * 512 + ic * 128 + 128],
                          XNT[:, kc * T + tc * 512:kc * T + tc * 512 + 512],
                          [wf_toks, xnt_toks]) for kc in range(8)]
                mt = mm_group(p, banks, bank, banks.f32(bank)[:, 0:512], terms)
                wf_rd.append(mt)

                def fg2(e, ic=ic, tc=tc, bank=bank):
                    return e.activation(HM[:, ic * T + tc * 512:
                                           ic * T + tc * 512 + 512],
                                        banks.f32(bank)[:, 0:512], AF.Gelu)

                hm_toks.append(bank_read(p, banks, bank, ACT, fg2,
                                         deps=_flat([mt, hm_rd]), async_=True))
        hm_rd = []
        wp_toks = []
        for srow in range(8):
            b = srow % 2
            ldw = p.dma(POOL, STGM[b][:, 0:512], wpr[ex, srow * 128:(srow + 1) * 128, :],
                        deps=_flat([wp_rd if srow < 2 else [],
                                    wp_toks[-2:] if srow >= 2 else [], s5_toks]))
            trs = [transpose_op(p, banks, 5,
                                banks.bf16(5)[:, kk * 128:(kk + 1) * 128],
                                STGM[b][:, kk * 128:(kk + 1) * 128], IDB,
                                deps=[ldw, c_idb]) for kk in range(4)]

            def fwev2(e, srow=srow):
                dst = WPT.rearrange("p (k c) -> p k c", k=4)[
                    :, :, srow * 128:(srow + 1) * 128]
                srcv = banks.bf16(5)[:, 0:512].rearrange("p (k c) -> p k c", k=4)
                return e.tensor_copy(dst, srcv)

            wp_toks.append(bank_read(p, banks, 5, DVE, fwev2,
                                     deps=_flat([trs, wp_rd])))
        wp_rd = []
        for ti in range(NT):
            for half in range(2):
                bank = 6 + (ti % 1) + half if False else 6 + half
                terms = [(HM[:, ic * T + ti * 128:ic * T + ti * 128 + 128],
                          WPT[:, ic * 1024 + half * 512:ic * 1024 + half * 512 + 512],
                          [hm_toks, wp_toks]) for ic in range(4)]
                mt = mm_group(p, banks, bank, banks.f32(bank)[:, 0:512], terms)
                wp_rd.append(mt)
                hm_rd.append(mt)

                def fxa(e, ti=ti, half=half, ex=ex, bank=bank):
                    Xsl = X[:, ti * D + half * 512:ti * D + half * 512 + 512]
                    return e.scalar_tensor_tensor(
                        Xsl, banks.f32(bank)[:, 0:512],
                        GT[:, ti * 8 + ex:ti * 8 + ex + 1], Xsl,
                        op0=ALU.mult, op1=ALU.add)

                moe_adds.append(bank_read(p, banks, bank, DVE, fxa,
                                          deps=_flat([mt, gate_toks[ti],
                                                      s5_toks, st["ld_x"]]),
                                          async_=True))
    if st.get("dbg") is not None:
        dbg = st["dbg"]
        for nm, ap in (("MHM", HM.bitcast(F32)), ("MWF", WFCT.bitcast(F32)),
                       ("MWP", WPT.bitcast(F32))):
            if nm in dbg["outs"]:
                td = p.dma(POOL, dbg["outs"][nm], ap, deps=_flat([moe_adds]))
                dbg.setdefault("dump_toks", []).append(td)
                moe_adds = list(moe_adds) + [td]
    _ys, _yd = tok_major_pair(X, y, NT)
    st_tok = p.dma(POOL, _yd, _ys, deps=_flat([moe_adds, s5_toks]))
    p.op(POOL, lambda e: e.nop(), deps=[st_tok])
    return st_tok


def build_block2(debug=False):
    nc = bass.Bass("TRN2", target_bir_lowering=False)
    x = nc.dram_tensor("x", [T, D], F32, kind="ExternalInput").ap()
    prm = {}
    for d, pfx in enumerate(("fwd", "bwd")):
        P = {}
        for nm, sh in (("log_real", [256]), ("imag", [256]), ("log_Delta", [256]),
                       ("D", [1024]), ("B_real", [256, 1024]),
                       ("B_imag", [256, 1024]), ("C_real", [1024, 256]),
                       ("C_imag", [1024, 256])):
            P[nm] = nc.dram_tensor(pfx + "_" + nm, sh, F32,
                                   kind="ExternalInput").ap()
        prm[d] = P
    wg = nc.dram_tensor("w_gate", [8, D], F32, kind="ExternalInput").ap()
    wfc = nc.dram_tensor("w_fc", [8, 512, D], F32, kind="ExternalInput").ap()
    wpr = nc.dram_tensor("w_proj", [8, D, 512], F32, kind="ExternalInput").ap()
    y = nc.dram_tensor("y", [T, D], F32, kind="ExternalOutput").ap()
    env = std_env(nc, 47104)
    p = Prog(nc)
    AR, ST, SCR = env["AR"], env["ST"], env["SCR"]
    f32, bf16, i32 = arena_views(AR)
    c_all = emit_consts(p, env["IDB"], env["IDF"], env["TRI"], env["ONES32"],
                        env["ONESBF"])
    X = f32(B0_X, NT * D)
    XN = f32(B0_R2, NT * D)
    XNT = bf16(B0_R3, 8192)
    Xs, xs = tok_major_pair(X, x, NT)
    ld_x = p.dma(SP, Xs, xs)
    ln_toks = emit_ln(p, X, XN, ST, SC_LN, SCR, ld_x)
    STG2 = [bf16(B0_WK + 5120, 512), bf16(B0_WK + 5632, 512)]
    xnt_toks = emit_cast_transpose(
        p, banks_ := env["banks"],
        [XN[:, i * D:(i + 1) * D] for i in range(NT)], XNT,
        STG2, env["IDB"], [[ln_toks[i], c_all[0]] for i in range(NT)])
    dbg = None
    if debug:
        shapes = {"BU00": [128, 2048], "BU10": [128, 2048],
                  "STP": [128, 320], "XSB0": [128, 1024], "XSB1": [128, 1024],
                  "X0": [128, NT * D], "X1": [128, NT * D], "GT": [128, 128],
                  "BCT": [128, 4096], "VRE0": [128, 2048], "VIM0": [128, 2048],
                  "DEC0": [128, 2048], "COSL": [128, 512], "SINL": [128, 512],
                  "MHM": [128, 4096], "MWF": [128, 2048], "MWP": [128, 2048]}
        outs = {nm: nc.dram_tensor("d_" + nm, sh, F32, kind="ExternalOutput").ap()
                for nm, sh in shapes.items()}
        dbg = {"outs": outs}
    import os as _os
    st = {"X": X, "XN": XN, "XNT": XNT, "consts": c_all, "ld_x": ld_x,
          "ln_toks": ln_toks, "xnt_toks": xnt_toks, "dbg": dbg,
          "upto": _os.environ.get("B2UPTO", "full") if debug else "full"}
    s5_toks, scan_prev = emit_block2_s5(p, env, st, prm)
    if st["upto"] == "full":
        emit_block2_moe(p, env, st, wg, wfc, wpr, y, _flat([s5_toks, scan_prev]))
    else:
        _ys, _yd = tok_major_pair(X, y, NT)
        stk = p.dma(POOL, _yd, _ys, deps=_flat([s5_toks, scan_prev, ld_x]))
        p.op(POOL, lambda e: e.nop(), deps=[stk])
    p.emit(env["sems"])
    return nc

EPS3 = 1.1920929e-07
LI3 = 0.2
SC3 = 0.044194173824159216

C3_SSQ, C3_RST1, C3_RST3 = 0, 16, 32
C3_MS8, C3_LAMT, C3_NEGL = 48, 56, 60
C3_B1, C3_B3 = 64, 80
C3_GB, C3_M8, C3_SM = 96, 104, 112


def emit_block3_attn(p, env, st, prm, dram):
    AR, ST, SCR = env["AR"], env["ST"], env["SCR"]
    ONES32, IDB = env["ONES32"], env["IDB"]
    banks = env["banks"]
    f32, bf16, i32 = arena_views(AR)
    X, XN = st["X"], st["XN"]
    ln_toks, ld_x = st["ln_toks"], st["ld_x"]
    c_idb, c_o32 = st["consts"][0], st["consts"][3]
    QP2, KP2, VP2, AT2, XND = (dram["QP2"], dram["KP2"], dram["VP2"],
                               dram["AT2"], dram["XND"])
    XN3T = bf16(32768, 8192)
    G1R = f32(45056, 1024)
    WQPT = bf16(40960, 4096)
    WSTG = [bf16(45056, 512), bf16(45568, 512)]
    STG3 = [bf16(46080, 512), bf16(46592, 512)]
    PSTG = [bf16(46080, 256), bf16(46336, 256)]
    scr_chain = []

    xs_, xd_ = tok_major_pair(XN, XND, NT)
    t_xnd = p.dma(POOL, xd_, xs_, deps=_flat([ln_toks]))
    st["t_xnd"] = t_xnd

    sq_toks = []
    for i in range(NT):
        t1 = p.op(ACT, (lambda i=i: (lambda e: e.activation(
            SCR[:, 0:D], XN[:, i * D:(i + 1) * D], AF.Square)))(),
            deps=_flat([ln_toks[i], scr_chain]), async_=True)

        t2 = p.op(DVE, (lambda i=i: (lambda e: e.tensor_reduce(
            ST[:, C3_SSQ + i:C3_SSQ + i + 1], SCR[:, 0:D], axis=AX,
            op=ALU.add)))(), deps=[t1], async_=True)
        scr_chain = [t2]
        sq_toks.append(t2)
    t3 = p.op(DVE, lambda e: e.tensor_scalar(
        ST[:, C3_RST1:C3_RST1 + 16], ST[:, C3_SSQ:C3_SSQ + 16],
        1.0 / D, EPS3, op0=ALU.mult, op1=ALU.add), deps=sq_toks, async_=True)
    t4 = p.op(ACT, lambda e: e.activation(
        ST[:, C3_RST1:C3_RST1 + 16], ST[:, C3_RST1:C3_RST1 + 16], AF.Ln),
        deps=[t3], async_=True)
    t5 = p.op(DVE, lambda e: e.tensor_scalar(
        ST[:, C3_RST1:C3_RST1 + 16], ST[:, C3_RST1:C3_RST1 + 16], -0.5, None,
        op0=ALU.mult), deps=[t4], async_=True)
    t_rst1 = p.op(ACT, lambda e: e.activation(
        ST[:, C3_RST1:C3_RST1 + 16], ST[:, C3_RST1:C3_RST1 + 16], AF.Exp),
        deps=[t5], async_=True)

    ldg1 = p.dma(SP, G1R[0:1, :], prm["g1"].rearrange("(one f) -> one f",
                                                      one=1))
    g1toks = []
    for c in range(2):
        war = banks.war(6 + c)

        def fg(e, c=c):
            return e.matmul(banks.f32(6 + c)[:, 0:512], ONES32[0:1, 0:128],
                            G1R[0:1, c * 512:(c + 1) * 512], start=True,
                            stop=True, skip_group_check=True)

        tm = p.op(PE, fg, deps=[ldg1, c_o32] + war)
        banks.wrote(6 + c, tm)

        def fe(e, c=c):
            return e.tensor_copy(G1R[:, c * 512:(c + 1) * 512],
                                 banks.f32(6 + c)[:, 0:512])

        g1toks.append(bank_read(p, banks, 6 + c, DVE, fe, deps=[ldg1]))

    LPR = ST[0:1, C3_LAMT:C3_LAMT + 4]
    ldlp = p.dma(SP, LPR, prm["lambda_params"].rearrange("(one f) -> one f",
                                                         one=1))
    tl1 = p.op(DVE, lambda e: e.tensor_tensor(
        ST[0:1, C3_LAMT:C3_LAMT + 1], LPR[:, 0:1], LPR[:, 1:2],
        op=ALU.mult), deps=[ldlp], async_=True)
    tl2 = p.op(DVE, lambda e: e.tensor_tensor(
        ST[0:1, C3_LAMT + 1:C3_LAMT + 2], LPR[:, 2:3], LPR[:, 3:4],
        op=ALU.mult), deps=[ldlp, tl1], async_=True)
    tl3 = p.op(ACT, lambda e: e.activation(
        ST[0:1, C3_LAMT:C3_LAMT + 2], ST[0:1, C3_LAMT:C3_LAMT + 2], AF.Exp),
        deps=[tl2], async_=True)
    tl4 = p.op(DVE, lambda e: e.tensor_tensor(
        ST[0:1, C3_LAMT:C3_LAMT + 1], ST[0:1, C3_LAMT + 1:C3_LAMT + 2],
        ST[0:1, C3_LAMT:C3_LAMT + 1], op=ALU.subtract), deps=[tl3],
        async_=True)
    tl5 = p.op(DVE, lambda e: e.tensor_scalar(
        ST[0:1, C3_LAMT:C3_LAMT + 1], ST[0:1, C3_LAMT:C3_LAMT + 1], -LI3,
        None, op0=ALU.add), deps=[tl4], async_=True)
    t_neg = bcast_col(p, banks, 5, ONES32, ST[0:1, C3_LAMT:C3_LAMT + 1],
                      ST[:, C3_NEGL:C3_NEGL + 1], deps=[tl5, c_o32])

    def prod3(i, stgb, deps):
        def f(e, i=i, stgb=stgb):
            return e.scalar_tensor_tensor(
                stgb, XN[:, i * D:(i + 1) * D],
                ST[:, C3_RST1 + i:C3_RST1 + i + 1], G1R,
                op0=ALU.mult, op1=ALU.mult)

        return p.op(DVE, f, deps=_flat([deps, t_rst1, g1toks, ln_toks[i],
                                        t_xnd]), async_=True)

    xn3_toks = emit_cast_transpose(
        p, banks, [None] * NT, XN3T, STG3, IDB,
        [[] for _ in range(NT)], produce=prod3)
    if st.get("dbg") is not None:
        dbg = st["dbg"]
        td = p.dma(POOL, dbg["outs"]["XN3T"], XN3T.bitcast(F32),
                   deps=_flat([xn3_toks]))
        dbg.setdefault("dump_toks", []).append(td)
        xn3_toks = list(xn3_toks) + [td]

    proj_specs = [("Wq", QP2, True, "elu"), ("Wk", KP2, True, "elu"),
                  ("Wv", VP2, False, "scale")]
    wq_rd = []
    pst_rd = {0: [], 1: []}
    st_toks = {"q": [], "k": [], "v": []}
    wstg_rd = {0: [], 1: []}
    prev_slab = []
    for wi, (wnm, dst2, permute, post) in enumerate(proj_specs):
        W = prm[wnm]
        wl_toks = []
        for b in range(8):
            sb = b % 2
            ldeps = _flat([xn3_toks if wi == 0 else [], wstg_rd[sb], wq_rd])
            if permute:
                W3 = W.rearrange("(h r) f -> h r f", h=16)
                ldw = [p.dma(POOL, WSTG[sb][r * 16:(r + 1) * 16, :],
                             W3[:, b * 8 + r, :], deps=ldeps)
                       for r in range(8)]
            else:
                ldw = p.dma(POOL, WSTG[sb][:], W[b * 128:(b + 1) * 128, :],
                            deps=ldeps)
            ldw = pool_fence(p, ldw)
            trs = [transpose_op(p, banks, kc % 8,
                                banks.bf16(kc % 8)[:, (b % 4) * 128:
                                                   (b % 4) * 128 + 128],
                                WSTG[sb][:, kc * 128:(kc + 1) * 128], IDB,
                                deps=[ldw, c_idb]) for kc in range(8)]
            wstg_rd[sb] = trs
            evs = []
            for kc in range(8):
                def fwev(e, kc=kc, b=b):
                    return e.tensor_copy(
                        WQPT[:, kc * 1024 + b * 128:kc * 1024 + b * 128 + 128],
                        banks.bf16(kc)[:, (b % 4) * 128:(b % 4) * 128 + 128])

                evs.append(bank_read(p, banks, kc, DVE, fwev,
                                     deps=_flat([trs[kc], wq_rd])))
            prev_slab = evs
            wl_toks.append(evs)
        wq_rd = []
        if st.get("dbg") is not None and wnm == "Wv":
            dbg = st["dbg"]
            td = p.dma(POOL, dbg["outs"]["WVPT"], WQPT.bitcast(F32),
                       deps=_flat([wl_toks]))
            dbg.setdefault("dump_toks", []).append(td)
            wq_rd.append(td)
        for i in range(NT):
            for c in range(2):
                bank = (i % 2) * 2 + c
                terms = [(XN3T[:, kc * T + i * 128:kc * T + i * 128 + 128],
                          WQPT[:, kc * 1024 + c * 512:kc * 1024 + c * 512 + 512],
                          [wl_toks, xn3_toks]) for kc in range(8)]
                mt = mm_group(p, banks, bank, banks.f32(bank)[:, 0:512], terms)
                wq_rd.append(mt)
                ps = (i * 2 + c) % 2
                pdep = pst_rd[ps]
                pst_rd[ps] = []
                if post == "elu":
                    e1 = bank_read(p, banks, bank, DVE,
                                   (lambda bank=bank: (lambda e: e.tensor_scalar(
                                       SCR[:, 0:512], banks.f32(bank)[:, 0:512],
                                       0.0, None, op0=ALU.min)))(),
                                   deps=_flat([mt, scr_chain]))
                    e2 = p.op(ACT, lambda e: e.activation(
                        SCR[:, 512:1024], SCR[:, 0:512], AF.Exp), deps=[e1],
                        async_=True)
                    e3 = bank_read(p, banks, bank, DVE,
                                   (lambda bank=bank: (lambda e: e.tensor_scalar(
                                       SCR[:, 0:512], banks.f32(bank)[:, 0:512],
                                       0.0, None, op0=ALU.max)))(),
                                   deps=[mt, e2])
                    e4 = p.op(DVE, lambda e: e.tensor_tensor(
                        SCR[:, 512:1024], SCR[:, 512:1024], SCR[:, 0:512],
                        op=ALU.add), deps=[e3], async_=True)
                    e5 = p.op(DVE, (lambda ps=ps: (lambda e: e.tensor_scalar(
                        PSTG[ps][:], SCR[:, 512:1024], -1.0, None,
                        op0=ALU.add)))(), deps=_flat([e4, pdep]), async_=True)
                    scr_chain = [e5]
                else:
                    e5 = bank_read(p, banks, bank, DVE,
                                   (lambda bank=bank, ps=ps: (lambda e:
                                    e.tensor_scalar(
                                        PSTG[ps][:], banks.f32(bank)[:, 0:512],
                                        SC3, None, op0=ALU.mult)))(),
                                   deps=_flat([mt, pdep]))
                tst = p.dma(SP, dst2[i * 128:(i + 1) * 128,
                                     c * 512:(c + 1) * 512],
                            PSTG[ps][:], deps=[e5])
                pst_rd[ps] = [tst]
                st_toks[wnm[1]].append(tst)

    LV = bf16(40960, 256)
    ATS = f32(41216, 512)
    G2R64 = f32(41728, 64)
    LKBD_ = [bf16(45056, 512), bf16(41792, 512)]
    LQBD_ = [bf16(45568, 512), bf16(42304, 512)]
    CS1 = bf16(46080, 512)
    CS2 = bf16(46592, 512)
    zt = [p.op(DVE, (lambda ap=ap: (lambda e: e.memset(ap, 0.0)))(),
               deps=_flat([st_toks["q"], st_toks["k"], st_toks["v"],
                           prev_slab]), async_=True)
          for ap in (LKBD_[0], LKBD_[1], LQBD_[0], LQBD_[1], CS1, CS2)]
    ldg2 = p.dma(SP, G2R64[0:1, :], prm["g2"].rearrange("(one f) -> one f",
                                                        one=1),
                 deps=_flat([st_toks["q"], st_toks["k"], st_toks["v"]]))
    war = banks.war(6)

    def fg2b(e):
        return e.matmul(banks.f32(6)[:, 0:64], ONES32[0:1, 0:128],
                        G2R64[0:1, :], start=True, stop=True,
                        skip_group_check=True)

    tg2 = p.op(PE, fg2b, deps=[ldg2, c_o32] + war)
    banks.wrote(6, tg2)
    t_g2r = bank_read(p, banks, 6, DVE, lambda e: e.tensor_scalar(
        G2R64[:], banks.f32(6)[:, 0:64], 1.0 - LI3, None, op0=ALU.mult))

    at_toks = []
    ld_prev = {"lk": [], "lq": [], "lv": []}
    for ch in range(32):
        base = ch * 64
        cb1, cb2 = (0, 1) if ch % 2 == 0 else (2, 3)
        mixb = 4 if ch % 2 == 0 else 5
        sc_toks = {"lk": [], "lq": []}
        for nm, BD_, SRC2 in (("lk", LKBD_, KP2), ("lq", LQBD_, QP2)):
            for hb in range(2):
                for q in range(4):
                    dst = BD_[hb][q * 32:(q + 1) * 32, :].rearrange(
                        "p (g s h) -> p g s h", g=16, s=4)[:, :, q, :]
                    src = SRC2[base:base + 64,
                               hb * 512:(hb + 1) * 512].rearrange(
                        "(g q) (e h) -> q e g h", q=4, e=32)[q]
                    td = p.dma(SP, dst, src,
                               deps=_flat([st_toks[nm[1]], ld_prev[nm]]))
                    sc_toks[nm].append((hb, td))
        ldv = p.dma(SP, LV[:].rearrange("p (G d) -> p G d", G=8),
                    VP2[base:base + 64, :].rearrange(
                        "(G q) (h d) -> (q h) G d", q=8, h=16),
                    deps=_flat([st_toks["v"], ld_prev["lv"]]))
        ld_prev = {"lk": [], "lq": [], "lv": []}
        cm_toks = {0: [], 1: []}
        for hb in range(2):
            bank = cb1 if hb == 0 else cb2
            war0 = banks.war(bank)
            lk_d = [td for h2, td in sc_toks["lk"] if h2 == hb]
            lq_d = [td for h2, td in sc_toks["lq"] if h2 == hb]
            mt = None
            for g in range(16):
                tp = (0, 0) if g % 2 == 0 else (0, 64)
                rows = slice(0, 64) if g % 2 == 0 else slice(64, 128)
                col = (g // 2) * 64

                def fcm(e, bank=bank, g=g, tp=tp, rows=rows, col=col,
                        hb=hb):
                    return e.matmul(
                        banks.f32(bank)[rows, col:col + 64],
                        LKBD_[hb][:, g * 64:(g + 1) * 64],
                        LQBD_[hb][:, g * 64:(g + 1) * 64],
                        start=True, stop=True, tile_position=tp,
                        skip_group_check=True)

                mt = p.op(PE, fcm, deps=_flat([lk_d, lq_d, zt])
                          + (war0 if g == 0 else []))
                cm_toks[hb].append(mt)
            banks.wrote(bank, mt)
            for t_ in cm_toks[hb][:-1]:
                banks.r[bank].append(t_)
        ld_prev["lk"] = [t for hb in range(2) for t in cm_toks[hb]]
        ld_prev["lq"] = ld_prev["lk"]
        ev_toks = []
        for hb, CS, bank in ((0, CS1, cb1), (1, CS2, cb2)):
            for part, qoff in ((slice(0, 64), 0), (slice(64, 128), 64)):
                if hb == 0:
                    def fev(e, CS=CS, bank=bank, part=part, qoff=qoff):
                        return e.tensor_copy(
                            CS[part].rearrange("p (G c) -> p G c", G=8)[
                                :, :, qoff:qoff + 64],
                            banks.f32(bank)[part].rearrange(
                                "p (G c) -> p G c", G=8))
                else:
                    def fev(e, CS=CS, bank=bank, part=part, qoff=qoff):
                        return e.tensor_scalar(
                            CS[part].rearrange("p (G c) -> p G c", G=8)[
                                :, :, qoff:qoff + 64],
                            banks.f32(bank)[part].rearrange(
                                "p (G c) -> p G c", G=8),
                            ST[part, C3_NEGL:C3_NEGL + 1], None, op0=ALU.mult)

                ev_toks.append(bank_read(p, banks, bank, DVE, fev,
                                         deps=_flat([cm_toks[hb], t_neg, zt])))
        if st.get("dbg") is not None and ch == 0:
            dbg = st["dbg"]
            for nm, ap in (("CS1", CS1.bitcast(F32)), ("CS2", CS2.bitcast(F32)),
                           ("LKB", LKBD_[0].bitcast(F32)),
                           ("LQB", LQBD_[0].bitcast(F32)),
                           ("LVD", LV.bitcast(F32))):
                td = p.dma(POOL, dbg["outs"][nm], ap,
                           deps=_flat([ev_toks, ldv] + [t for _, t in
                                                        sc_toks["lk"] +
                                                        sc_toks["lq"]]))
                dbg.setdefault("dump_toks", []).append(td)
                ev_toks = list(ev_toks) + [td]
        warm = banks.war(mixb)
        mx_toks = []
        for G in range(8):
            for hb, CS in ((0, CS1), (1, CS2)):
                def fmx(e, G=G, CS=CS, mixb=mixb, s=(hb == 0)):
                    return e.matmul(
                        banks.f32(mixb)[:, G * 64:(G + 1) * 64],
                        CS[:, G * 128:(G + 1) * 128],
                        LV[:, G * 64:(G + 1) * 64],
                        start=s, stop=(not s), skip_group_check=True)

                mx = p.op(PE, fmx, deps=_flat([ev_toks, ldv])
                          + (warm if G == 0 and hb == 0 else []))
                mx_toks.append(mx)
        banks.wrote(mixb, mx_toks[-1])
        for t_ in mx_toks[:-1]:
            banks.r[mixb].append(t_)
        ld_prev["lv"] = mx_toks
        r1 = bank_read(p, banks, mixb, ACT,
                       (lambda mixb=mixb: (lambda e: e.activation(
                           SCR[:, 1024:1536], banks.f32(mixb)[:, 0:512],
                           AF.Square)))(), deps=_flat([mx_toks, scr_chain]))
        r2 = p.op(DVE, lambda e: e.tensor_reduce(
            ST[:, C3_MS8:C3_MS8 + 8],
            SCR[:, 1024:1536].rearrange("p (j d) -> p j d", j=8), axis=AX,
            op=ALU.add), deps=[r1], async_=True)
        r3 = p.op(DVE, lambda e: e.tensor_scalar(
            ST[:, C3_MS8:C3_MS8 + 8], ST[:, C3_MS8:C3_MS8 + 8], 1.0 / 64,
            EPS3, op0=ALU.mult, op1=ALU.add), deps=[r2], async_=True)
        r4 = p.op(ACT, lambda e: e.activation(
            ST[:, C3_MS8:C3_MS8 + 8], ST[:, C3_MS8:C3_MS8 + 8], AF.Ln),
            deps=[r3], async_=True)
        r5 = p.op(DVE, lambda e: e.tensor_scalar(
            ST[:, C3_MS8:C3_MS8 + 8], ST[:, C3_MS8:C3_MS8 + 8], -0.5, None,
            op0=ALU.mult), deps=[r4], async_=True)
        r6 = p.op(ACT, lambda e: e.activation(
            ST[:, C3_MS8:C3_MS8 + 8], ST[:, C3_MS8:C3_MS8 + 8], AF.Exp),
            deps=[r5], async_=True)
        scr_chain = [r2]
        ap_toks = []
        for j in range(8):
            def fap(e, j=j, mixb=mixb):
                return e.scalar_tensor_tensor(
                    ATS[:, j * 64:(j + 1) * 64],
                    banks.f32(mixb)[:, j * 64:(j + 1) * 64],
                    ST[:, C3_MS8 + j:C3_MS8 + j + 1], G2R64,
                    op0=ALU.mult, op1=ALU.mult)

            ap_toks.append(bank_read(p, banks, mixb, DVE, fap,
                                     deps=_flat([r6, t_g2r, at_toks[-1:]])))
        tat = p.dma(POOL, AT2[base:base + 64, :].rearrange(
            "(G q) (h d) -> (q h) G d", q=8, h=16),
            ATS[:].rearrange("p (G d) -> p G d", G=8), deps=ap_toks)
        at_toks.append(tat)
    return {"at_toks": at_toks, "scr_chain": scr_chain}


def emit_block3_mlp(p, env, st, prm, dram, mix):
    AR, ST, SCR = env["AR"], env["ST"], env["SCR"]
    ONES32, IDB = env["ONES32"], env["IDB"]
    banks = env["banks"]
    f32, bf16, i32 = arena_views(AR)
    X = st["X"]
    c_idb, c_o32 = st["consts"][0], st["consts"][3]
    AT2 = dram["AT2"]
    at_toks = mix["at_toks"]
    scr_chain = list(mix["scr_chain"])
    ATT = f32(16384, 8192)
    AFM = bf16(24576, 4096)
    HFM = bf16(28672, 4096)
    W1T = bf16(32768, 4096)
    W2T = bf16(36864, 4096)
    G3R = f32(45056, 1024)
    WSTG = [bf16(46080, 512), bf16(46592, 512)]
    STGA = [bf16(40960, 512), bf16(41472, 512)]

    ldg3 = p.dma(SP, G3R[0:1, :], prm["g3"].rearrange("(one f) -> one f",
                                                      one=1),
                 deps=_flat([at_toks]))
    g3toks = []
    for c in range(2):
        war = banks.war(6 + c)

        def fg(e, c=c):
            return e.matmul(banks.f32(6 + c)[:, 0:512], ONES32[0:1, 0:128],
                            G3R[0:1, c * 512:(c + 1) * 512], start=True,
                            stop=True, skip_group_check=True)

        tm = p.op(PE, fg, deps=[ldg3, c_o32] + war)
        banks.wrote(6 + c, tm)

        def fe(e, c=c):
            return e.tensor_copy(G3R[:, c * 512:(c + 1) * 512],
                                 banks.f32(6 + c)[:, 0:512])

        g3toks.append(bank_read(p, banks, 6 + c, DVE, fe, deps=[ldg3]))

    wt_toks = {"W1": [], "W2": []}
    wstg_rd = {0: [], 1: []}
    for wnm, DSTW in (("W1", W1T), ("W2", W2T)):
        W = prm[wnm]
        for b in range(8):
            sb = b % 2
            ldw = p.dma(POOL, WSTG[sb][:], W[b * 128:(b + 1) * 128, :],
                        deps=_flat([wstg_rd[sb], mix["at_toks"][-1:]]))
            ldw = pool_fence(p, ldw)
            trs = [transpose_op(p, banks, kc % 8,
                                banks.bf16(kc % 8)[:, (b % 4) * 128:
                                                   (b % 4) * 128 + 128],
                                WSTG[sb][:, kc * 128:(kc + 1) * 128], IDB,
                                deps=[ldw, c_idb]) for kc in range(8)]
            wstg_rd[sb] = trs
            evs = []
            for kc in range(8):
                def fwev(e, kc=kc, b=b, DSTW=DSTW):
                    return e.tensor_copy(
                        DSTW[:, kc * 1024 + b * 128:kc * 1024 + b * 128 + 128],
                        banks.bf16(kc)[:, (b % 4) * 128:(b % 4) * 128 + 128])

                evs.append(bank_read(p, banks, kc, DVE, fwev, deps=[trs[kc]]))
            wt_toks[wnm].append(evs)

    at_fence = pool_fence(p, at_toks)
    x_adds = []
    afm_rd = []
    hfm_rd = []
    for H in range(2):
        ld_at = []
        for t_ in range(8):
            rows = H * 1024 + t_ * 128
            ld_at.append(p.dma(SP, ATT[:, t_ * D:(t_ + 1) * D],
                               AT2[rows:rows + 128, :],
                               deps=_flat([at_fence, x_adds])))
        a2_toks = []
        for t_ in range(8):
            s1 = p.op(ACT, (lambda t_=t_: (lambda e: e.activation(
                SCR[:, 0:D], ATT[:, t_ * D:(t_ + 1) * D], AF.Square)))(),
                deps=_flat([ld_at[t_], scr_chain]), async_=True)
            s2 = p.op(DVE, (lambda t_=t_: (lambda e: e.tensor_reduce(
                ST[:, C3_RST3 + t_:C3_RST3 + t_ + 1], SCR[:, 0:D], axis=AX,
                op=ALU.add)))(), deps=[s1], async_=True)
            s3 = p.op(DVE, (lambda t_=t_: (lambda e: e.tensor_scalar(
                ST[:, C3_RST3 + t_:C3_RST3 + t_ + 1],
                ST[:, C3_RST3 + t_:C3_RST3 + t_ + 1], 1.0 / D, EPS3,
                op0=ALU.mult, op1=ALU.add)))(), deps=[s2], async_=True)
            s4 = p.op(ACT, (lambda t_=t_: (lambda e: e.activation(
                ST[:, C3_RST3 + t_:C3_RST3 + t_ + 1],
                ST[:, C3_RST3 + t_:C3_RST3 + t_ + 1], AF.Ln)))(),
                deps=[s3], async_=True)
            s5 = p.op(DVE, (lambda t_=t_: (lambda e: e.tensor_scalar(
                ST[:, C3_RST3 + t_:C3_RST3 + t_ + 1],
                ST[:, C3_RST3 + t_:C3_RST3 + t_ + 1], -0.5, None,
                op0=ALU.mult)))(), deps=[s4], async_=True)
            s6 = p.op(ACT, (lambda t_=t_: (lambda e: e.activation(
                ST[:, C3_RST3 + t_:C3_RST3 + t_ + 1],
                ST[:, C3_RST3 + t_:C3_RST3 + t_ + 1], AF.Exp)))(),
                deps=[s5], async_=True)
            s7 = p.op(DVE, (lambda t_=t_: (lambda e: e.scalar_tensor_tensor(
                SCR[:, 0:D], G3R, ST[:, C3_RST3 + t_:C3_RST3 + t_ + 1],
                ATT[:, t_ * D:(t_ + 1) * D], op0=ALU.mult, op1=ALU.mult)))(),
                deps=_flat([s6, g3toks, s2]), async_=True)
            s8 = p.op(DVE, (lambda t_=t_: (lambda e: e.tensor_tensor(
                ATT[:, t_ * D:(t_ + 1) * D], ATT[:, t_ * D:(t_ + 1) * D],
                SCR[:, 0:D], op=ALU.add)))(), deps=[s7, s1], async_=True)
            scr_chain = [s8]
            a2_toks.append(s8)
            xt = H * 8 + t_
            x_adds.append(p.op(DVE, (lambda t_=t_, xt=xt: (lambda e:
                          e.tensor_tensor(
                              X[:, xt * D:(xt + 1) * D],
                              X[:, xt * D:(xt + 1) * D],
                              ATT[:, t_ * D:(t_ + 1) * D], op=ALU.add)))(),
                deps=_flat([s8, st["ld_x"]]), async_=True))
        afm_toks = emit_cast_transpose(
            p, banks, [ATT[:, t_ * D:(t_ + 1) * D] for t_ in range(8)],
            AFM, STGA, IDB, [[a2_toks[t_], afm_rd] for t_ in range(8)],
            dst_stride=1024)
        afm_rd = []
        hg_toks = []
        for fc in range(8):
            for tc in range(2):
                bank = (fc % 2) * 2 + tc
                terms = [(W1T[:, kc * 1024 + fc * 128:kc * 1024 + fc * 128 + 128],
                          AFM[:, kc * 1024 + tc * 512:kc * 1024 + tc * 512 + 512],
                          [wt_toks["W1"], afm_toks]) for kc in range(8)]
                mt = mm_group(p, banks, bank, banks.f32(bank)[:, 0:512], terms)
                afm_rd.append(mt)

                def fgl(e, fc=fc, tc=tc, bank=bank):
                    return e.activation(
                        HFM[:, fc * 1024 + tc * 512:fc * 1024 + tc * 512 + 512],
                        banks.f32(bank)[:, 0:512], AF.Gelu)

                hg_toks.append(bank_read(p, banks, bank, ACT, fgl,
                                         deps=_flat([mt, hfm_rd])))
        hfm_rd = []
        for tl in range(8):
            for c in range(2):
                bank = 4 + (tl % 2) * 2 + c
                terms = [(HFM[:, fc * 1024 + tl * 128:fc * 1024 + tl * 128 + 128],
                          W2T[:, fc * 1024 + c * 512:fc * 1024 + c * 512 + 512],
                          [wt_toks["W2"], hg_toks]) for fc in range(8)]
                mt = mm_group(p, banks, bank, banks.f32(bank)[:, 0:512], terms)
                hfm_rd.append(mt)
                xt = H * 8 + tl

                def fxa(e, xt=xt, c=c, bank=bank):
                    Xsl = X[:, xt * D + c * 512:xt * D + c * 512 + 512]
                    return e.tensor_tensor(Xsl, Xsl,
                                           banks.f32(bank)[:, 0:512],
                                           op=ALU.add)

                x_adds.append(bank_read(p, banks, bank, DVE, fxa,
                                        deps=_flat([mt, x_adds[-1:]])))
    return {"x_adds": x_adds, "scr_chain": scr_chain,
            "hfm_rd": hfm_rd, "afm_rd": afm_rd}


def emit_block3_ffn(p, env, st, prm, dram, mlp, y):
    AR, ST, SCR = env["AR"], env["ST"], env["SCR"]
    ONES32, ONESBF, IDB = env["ONES32"], env["ONESBF"], env["IDB"]
    banks = env["banks"]
    f32, bf16, i32 = arena_views(AR)
    X = st["X"]
    c_idb, c_o32 = st["consts"][0], st["consts"][3]
    XND = dram["XND"]
    x_adds = list(mlp["x_adds"])
    scr_chain = list(mlp["scr_chain"])
    prev_mm = _flat([mlp["hfm_rd"], mlp["afm_rd"]])
    XNT = bf16(32768, 8192)
    HM = bf16(16384, 11264)
    W1S = bf16(27648, 1024)
    W1S2 = bf16(28672, 1024)
    W2S = bf16(40960, 5632)
    GT = f32(29696, 128)
    GWT = bf16(29824, 32)
    B2R = bf16(29856, 512)
    STGF = [bf16(30976, 512), bf16(31488, 512)]

    def prodf(i, stgb, deps):
        td = p.dma(POOL, stgb, XND[i * 128:(i + 1) * 128, :],
                   deps=_flat([deps, prev_mm, st["t_xnd"]]))
        return pool_fence(p, td)

    xnt_toks = emit_cast_transpose(
        p, banks, [None] * NT, XNT, STGF, IDB, [[] for _ in range(NT)],
        produce=prodf, war_dst=prev_mm)

    ldgw = p.dma(POOL, STGF[0][0:8, :], prm["gate_w"][:, :],
                 deps=_flat([xnt_toks]))
    ldgw = pool_fence(p, ldgw)
    gw_toks = []
    for kc in range(8):
        tr = transpose_op(p, banks, 6, banks.bf16(6)[:, 0:8],
                          STGF[0][0:8, kc * 128:(kc + 1) * 128], IDB[0:8, 0:8],
                          deps=[ldgw, c_idb])

        def fgv(e, kc=kc):
            return e.tensor_copy(GWT[:, kc * 8:(kc + 1) * 8],
                                 banks.bf16(6)[:, 0:8])

        gw_toks.append(bank_read(p, banks, 6, DVE, fgv, deps=[tr]))
    ldgb = p.dma(SP, ST[0:1, C3_GB:C3_GB + 8],
                 prm["gate_b"].rearrange("(one f) -> one f", one=1))
    war = banks.war(7)

    def fgbb(e):
        return e.matmul(banks.f32(7)[:, 0:8], ONES32[0:1, 0:128],
                        ST[0:1, C3_GB:C3_GB + 8], start=True, stop=True,
                        skip_group_check=True)

    tgb = p.op(PE, fgbb, deps=[ldgb, c_o32] + war)
    banks.wrote(7, tgb)
    t_gbr = bank_read(p, banks, 7, DVE, lambda e: e.tensor_copy(
        ST[:, C3_GB:C3_GB + 8], banks.f32(7)[:, 0:8]))

    gate_toks = []
    for ti in range(NT):
        bank = 6 + ti % 2
        terms = [(XNT[:, kc * T + ti * 128:kc * T + ti * 128 + 128],
                  GWT[:, kc * 8:(kc + 1) * 8], [gw_toks, xnt_toks])
                 for kc in range(8)]
        mt = mm_group(p, banks, bank, banks.f32(bank)[:, 0:8], terms)
        g1 = bank_read(p, banks, bank, DVE,
                       (lambda bank=bank: (lambda e: e.max(
                           ST[:, C3_M8:C3_M8 + 8],
                           banks.f32(bank)[:, 0:8])))(), deps=[mt])
        g2 = p.op(DVE, lambda e: e.tensor_scalar(
            ST[:, C3_SM:C3_SM + 1], ST[:, C3_M8:C3_M8 + 1], -1.0, None,
            op0=ALU.mult), deps=[g1], async_=True)
        g3 = bank_read(p, banks, bank, ACT,
                       (lambda bank=bank: (lambda e: e.activation(
                           SCR[:, 0:8], banks.f32(bank)[:, 0:8], AF.Exp,
                           bias=ST[:, C3_SM:C3_SM + 1])))(),
                       deps=_flat([g2, scr_chain]))
        g4 = p.op(DVE, lambda e: e.tensor_reduce(
            ST[:, C3_SM + 1:C3_SM + 2], SCR[:, 0:8], axis=AX, op=ALU.add),
            deps=[g3], async_=True)
        g5 = p.op(DVE, lambda e: e.reciprocal(
            ST[:, C3_SM + 2:C3_SM + 3], ST[:, C3_SM + 1:C3_SM + 2]),
            deps=[g4], async_=True)
        g6 = p.op(DVE, lambda e: e.scalar_tensor_tensor(
            SCR[:, 8:16], SCR[:, 0:8], ST[:, C3_SM + 2:C3_SM + 3],
            ST[:, C3_GB:C3_GB + 8], op0=ALU.mult, op1=ALU.add),
            deps=_flat([g5, t_gbr, g3]), async_=True)
        g7 = p.op(DVE, lambda e: e.max(ST[:, C3_M8:C3_M8 + 8], SCR[:, 8:16]),
                  deps=[g6], async_=True)
        g8 = p.op(DVE, lambda e: e.tensor_scalar(
            SCR[:, 16:24], SCR[:, 8:16], ST[:, C3_M8 + 1:C3_M8 + 2], None,
            op0=ALU.is_ge), deps=[g7, g6], async_=True)
        g9 = p.op(DVE, (lambda ti=ti: (lambda e: e.scalar_tensor_tensor(
            GT[:, ti * 8:(ti + 1) * 8], SCR[:, 0:8],
            ST[:, C3_SM + 2:C3_SM + 3], SCR[:, 16:24],
            op0=ALU.mult, op1=ALU.mult)))(), deps=[g8, g5, g3], async_=True)
        scr_chain = [g9]
        gate_toks.append(g9)

    w1s_rd = {0: [], 1: []}
    w2s_rd = []
    hm_rd = []
    ldb_prev = []
    for ex in range(9):
        shared = ex == 8
        if shared:
            w1d, w3d, w2d = prm["sw1"], prm["sw3"], prm["sw2"]
            b1d, b3d, b2d = prm["sb1"], prm["sb3"], prm["sb2"]
        else:
            w1d, w3d, w2d = (prm["w1"][ex], prm["w3"][ex], prm["w2"][ex])
            b1d, b3d, b2d = (prm["b1"][ex], prm["b3"][ex], prm["b2"][ex])
        lb1 = [p.dma(SP, ST[:, C3_B1 + c:C3_B1 + c + 1],
                     b1d[c * 128:(c + 1) * 128].rearrange(
                         "(p one) -> p one", one=1), deps=ldb_prev)
               for c in range(11)]
        lb3 = [p.dma(SP, ST[:, C3_B3 + c:C3_B3 + c + 1],
                     b3d[c * 128:(c + 1) * 128].rearrange(
                         "(p one) -> p one", one=1), deps=ldb_prev)
               for c in range(11)]
        lb2 = p.dma(POOL, B2R[0:1, 0:1024],
                    b2d.rearrange("(one f) -> one f", one=1), deps=ldb_prev)
        lb2 = pool_fence(p, lb2)
        hm_toks = []
        for hc in range(11):
            sb = hc % 2
            W1Sb = W1S if sb == 0 else W1S2
            ld1 = [p.dma(POOL, W1Sb[:, kc * 128:kc * 128 + 128],
                         w1d[kc * 128:(kc + 1) * 128,
                             hc * 128:(hc + 1) * 128],
                         deps=_flat([w1s_rd[sb]]))
                   for kc in range(8)]
            ld3 = [p.dma(POOL, W1Sb[:, 1024 + kc * 128:1024 + kc * 128 + 128],
                         w3d[kc * 128:(kc + 1) * 128,
                             hc * 128:(hc + 1) * 128],
                         deps=_flat([w1s_rd[sb]]))
                   for kc in range(8)]
            ldf = pool_fence(p, [ld1, ld3])
            ld1 = ld3 = [ldf]
            w1s_rd[sb] = []
            for tc in range(4):
                b_a = tc % 2
                b_b = 2 + tc % 2
                terms1 = [(W1Sb[:, kc * 128:(kc + 1) * 128],
                           XNT[:, kc * T + tc * 512:kc * T + tc * 512 + 512],
                           [ld1, xnt_toks]) for kc in range(8)]
                m1 = mm_group(p, banks, b_a, banks.f32(b_a)[:, 0:512], terms1)
                terms3 = [(W1Sb[:, 1024 + kc * 128:1024 + kc * 128 + 128],
                           XNT[:, kc * T + tc * 512:kc * T + tc * 512 + 512],
                           [ld3, xnt_toks]) for kc in range(8)]
                m3 = mm_group(p, banks, b_b, banks.f32(b_b)[:, 0:512], terms3)
                w1s_rd[sb].extend([m1, m3])
                u1 = bank_read(p, banks, b_a, DVE,
                               (lambda b_a=b_a, hc=hc: (lambda e:
                                e.tensor_scalar(
                                    SCR[:, 1024:1536],
                                    banks.f32(b_a)[:, 0:512],
                                    ST[:, C3_B1 + hc:C3_B1 + hc + 1], None,
                                    op0=ALU.add)))(),
                               deps=_flat([m1, lb1, scr_chain]))

                def fu2(e, b_b=b_b, hc=hc):
                    return e.scalar_tensor_tensor(
                        SCR[:, 1536:2048], banks.f32(b_b)[:, 0:512],
                        ST[:, C3_B3 + hc:C3_B3 + hc + 1], SCR[:, 1024:1536],
                        op0=ALU.add, op1=ALU.mult)

                u2 = bank_read(p, banks, b_b, DVE, fu2,
                               deps=_flat([m3, lb3, u1]))

                def fhm(e, hc=hc, tc=tc):
                    return e.activation(
                        HM[:, hc * T + tc * 512:hc * T + tc * 512 + 512],
                        SCR[:, 1536:2048], AF.Silu)

                hm = p.op(ACT, fhm, deps=_flat([u2, hm_rd]), async_=True)
                hm_toks.append(hm)
                scr_chain = [hm, u2]
        hm_rd = []
        ld2 = [p.dma(POOL, W2S[:, ic * 1024:(ic + 1) * 1024],
                     w2d[ic * 128:(ic + 1) * 128, :],
                     deps=_flat([w2s_rd]))
               for ic in range(11)]
        ld2f = pool_fence(p, ld2)
        ld2 = [ld2f] * 11
        w2s_rd = []
        for tl in range(NT):
            for c in range(2):
                bank = 4 + (tl % 2) * 2 + c
                war0 = banks.war(bank)
                mts = []
                for ic in range(11):
                    def fm2(e, ic=ic, tl=tl, c=c, bank=bank, s=(ic == 0)):
                        return e.matmul(
                            banks.f32(bank)[:, 0:512],
                            HM[:, ic * T + tl * 128:ic * T + tl * 128 + 128],
                            W2S[:, ic * 1024 + c * 512:ic * 1024 + c * 512 + 512],
                            start=s, stop=False, skip_group_check=True)

                    mts.append(p.op(PE, fm2, deps=_flat(
                        [hm_toks, ld2[ic]]) + (war0 if ic == 0 else [])))

                def fmb(e, tl=tl, c=c, bank=bank):
                    return e.matmul(
                        banks.f32(bank)[:, 0:512], ONESBF[0:1, 0:128],
                        B2R[0:1, c * 512:(c + 1) * 512], start=False,
                        stop=True, skip_group_check=True)

                mtb = p.op(PE, fmb, deps=_flat([mts[-1], lb2]))
                banks.wrote(bank, mtb)
                w2s_rd.append(mtb)
                hm_rd.append(mtb)
                if shared:
                    def fxa(e, tl=tl, c=c, bank=bank):
                        Xsl = X[:, tl * D + c * 512:tl * D + c * 512 + 512]
                        return e.tensor_tensor(
                            Xsl, Xsl, banks.f32(bank)[:, 0:512], op=ALU.add)
                else:
                    def fxa(e, tl=tl, c=c, bank=bank, ex=ex):
                        Xsl = X[:, tl * D + c * 512:tl * D + c * 512 + 512]
                        return e.scalar_tensor_tensor(
                            Xsl, banks.f32(bank)[:, 0:512],
                            GT[:, tl * 8 + ex:tl * 8 + ex + 1], Xsl,
                            op0=ALU.mult, op1=ALU.add)

                x_adds.append(bank_read(
                    p, banks, bank, DVE, fxa,
                    deps=_flat([mtb, gate_toks[tl] if not shared else [],
                                x_adds[-1:], st["ld_x"]])))
        ldb_prev = _flat([hm_toks, w2s_rd])
    if st.get("dbg") is not None:
        dbg = st["dbg"]
        for nm, ap in (("GT3", GT), ("HM3", HM.bitcast(F32))):
            td = p.dma(POOL, dbg["outs"][nm], ap, deps=_flat([x_adds]))
            dbg.setdefault("dump_toks", []).append(td)
            x_adds = list(x_adds) + [td]
    _ys, _yd = tok_major_pair(X, y, NT)
    st_tok = p.dma(POOL, _yd, _ys, deps=_flat([x_adds]))
    p.op(POOL, lambda e: e.nop(), deps=[st_tok])
    return st_tok


def build_block3(debug=False):
    nc = bass.Bass("TRN2", target_bir_lowering=False)
    x = nc.dram_tensor("x", [T, D], F32, kind="ExternalInput").ap()
    prm = {}
    for nm, sh in (("Wq", [D, D]), ("Wk", [D, D]), ("Wv", [D, D]),
                   ("lambda_params", [4]), ("g1", [D]), ("g2", [64]),
                   ("g3", [D]), ("W1", [D, D]), ("W2", [D, D]),
                   ("gate_w", [8, D]), ("gate_b", [8]),
                   ("w1", [8, D, 1408]), ("b1", [8, 1408]),
                   ("w3", [8, D, 1408]), ("b3", [8, 1408]),
                   ("w2", [8, 1408, D]), ("b2", [8, D]),
                   ("sw1", [D, 1408]), ("sb1", [1408]),
                   ("sw3", [D, 1408]), ("sb3", [1408]),
                   ("sw2", [1408, D]), ("sb2", [D])):
        prm[nm] = nc.dram_tensor("L3_" + nm, sh, F32,
                                 kind="ExternalInput").ap()
    y = nc.dram_tensor("y", [T, D], F32, kind="ExternalOutput").ap()
    dram = {}
    knd = "ExternalOutput" if debug else "Internal"
    for nm, sh, dt in (("QP2", [T, D], BF16), ("KP2", [T, D], BF16),
                       ("VP2", [T, D], BF16), ("AT2", [T, D], F32),
                       ("XND", [T, D], BF16)):
        dram[nm] = nc.dram_tensor("s_" + nm, sh, dt, kind=knd).ap()
    env = std_env(nc, 47104)
    p = Prog(nc)
    AR, ST, SCR = env["AR"], env["ST"], env["SCR"]
    f32, bf16, i32 = arena_views(AR)
    c_all = emit_consts(p, env["IDB"], env["IDF"], env["TRI"], env["ONES32"],
                        env["ONESBF"])
    X = f32(B0_X, NT * D)
    XN = f32(B0_R2, NT * D)
    Xs, xs = tok_major_pair(X, x, NT)
    ld_x = p.dma(SP, Xs, xs)
    ln_toks = emit_ln(p, X, XN, ST, 128, SCR, ld_x)
    import os as _os
    dbg = None
    if debug:
        shapes = {"XN3T": [128, 8192], "WVPT": [128, 4096],
                  "CS1": [128, 512], "CS2": [128, 512], "LKB": [128, 512],
                  "LQB": [128, 512], "LVD": [128, 256],
                  "GT3": [128, 128], "HM3": [128, 11264]}
        outs = {nm: nc.dram_tensor("d_" + nm, sh, F32,
                                   kind="ExternalOutput").ap()
                for nm, sh in shapes.items()}
        dbg = {"outs": outs}
    if _os.environ.get("DBGMIN"):
        dbg = None
    st = {"X": X, "XN": XN, "consts": c_all, "ld_x": ld_x,
          "ln_toks": ln_toks, "dbg": dbg,
          "upto": _os.environ.get("B3UPTO", "full") if debug else "full"}
    mix = emit_block3_attn(p, env, st, prm, dram)
    if st["upto"] == "proj":
        p.op(POOL, lambda e: e.nop(), deps=_flat([mix["at_toks"]]))
        _ys, _yd = tok_major_pair(X, y, NT)
        stk = p.dma(POOL, _yd, _ys, deps=_flat([mix["at_toks"], ld_x]))
        p.op(POOL, lambda e: e.nop(), deps=[stk])
        p.emit(env["sems"])
        return nc
    mlp = emit_block3_mlp(p, env, st, prm, dram, mix)
    if st["upto"] == "attn":
        _ys, _yd = tok_major_pair(X, y, NT)
        stk = p.dma(POOL, _yd, _ys, deps=_flat([mlp["x_adds"], ld_x]))
        p.op(POOL, lambda e: e.nop(), deps=[stk])
        p.emit(env["sems"])
        return nc
    emit_block3_ffn(p, env, st, prm, dram, mlp, y)
    p.emit(env["sems"])
    return nc



def kernel(**inputs):
    import numpy as _np
    from concourse.bass_utils import run_bass_kernel_spmd as _run

    ncores = 8

    def asf(a):
        return _np.ascontiguousarray(_np.asarray(a), dtype=_np.float32)

    x = asf(inputs["L0_attn_x"])

    def launch(nc, common, xcur):
        maps = [dict(common, x=xcur[c]) for c in range(ncores)]
        res = _run(nc, maps, core_ids=list(range(ncores)))
        return _np.stack([_np.asarray(res.results[c]["y"])
                          for c in range(ncores)])

    c0 = {"W_qkv": asf(inputs["L0_attn_W_qkv"]),
          "w1": asf(inputs["L0_ffn_w1"]), "w2": asf(inputs["L0_ffn_w2"])}
    x = launch(build_block0(), c0, x)
    c1 = {"Wq": asf(inputs["L1_attn_Wq"]), "Wk": asf(inputs["L1_attn_Wk"]),
          "Wv": asf(inputs["L1_attn_Wv"]), "Wo": asf(inputs["L1_attn_Wo"]),
          "bo": asf(inputs["L1_attn_bo"]),
          "w_gate": asf(inputs["L1_ffn_w_gate"]),
          "w_up": asf(inputs["L1_ffn_w_up"]),
          "w_down": asf(inputs["L1_ffn_w_down"])}
    x = launch(build_block1(), c1, x)
    c2 = {"w_gate": asf(inputs["L2_ffn_w_gate"]),
          "w_fc": asf(inputs["L2_ffn_w_fc"]),
          "w_proj": asf(inputs["L2_ffn_w_proj"]),
          "fwd_log_real": asf(inputs["L2_attn_fwd_log_real"]),
          "fwd_imag": asf(inputs["L2_attn_fwd_imag"]),
          "fwd_log_Delta": asf(inputs["L2_attn_fwd_log_Delta"]),
          "fwd_D": asf(inputs["L2_attn_fwd_D"]),
          "fwd_B_real": asf(inputs["L2_attn_fwd_B_real"]),
          "fwd_B_imag": asf(inputs["L2_attn_fwd_B_imag"]),
          "fwd_C_real": asf(inputs["L2_attn_fwd_C_real"]),
          "fwd_C_imag": asf(inputs["L2_attn_fwd_C_imag"]),
          "bwd_log_real": asf(inputs["L2_attn_bwd_log_real"]),
          "bwd_imag": asf(inputs["L2_attn_bwd_imag"]),
          "bwd_log_Delta": asf(inputs["L2_attn_bwd_log_Delta"]),
          "bwd_D": asf(inputs["L2_attn_bwd_D"]),
          "bwd_B_real": asf(inputs["L2_attn_bwd_B_real"]),
          "bwd_B_imag": asf(inputs["L2_attn_bwd_B_imag"]),
          "bwd_C_real": asf(inputs["L2_attn_bwd_C_real"]),
          "bwd_C_imag": asf(inputs["L2_attn_bwd_C_imag"])}
    x = launch(build_block2(), c2, x)
    c3 = {}
    for nm in ("Wq", "Wk", "Wv", "lambda_params", "g1", "g2", "g3", "W1",
               "W2"):
        c3["L3_" + nm] = asf(inputs["L3_attn_" + nm])
    for nm in ("gate_w", "gate_b", "w1", "b1", "w3", "b3", "w2", "b2",
               "sw1", "sb1", "sw3", "sb3", "sw2", "sb2"):
        c3["L3_" + nm] = asf(inputs["L3_ffn_" + nm])
    x = launch(build_block3(), c3, x)
    return x



def build_all():
    nc = bass.Bass("TRN2", target_bir_lowering=False)
    ti = lambda nm, sh: nc.dram_tensor(nm, sh, F32, kind="ExternalInput").ap()
    x = ti("x", [T, D])
    b0W = ti("W_qkv", [D, 3 * D])
    b0w1 = ti("b0w1", [4 * D, D])
    b0w2 = ti("b0w2", [D, 4 * D])
    b1Wq, b1Wk, b1Wv = ti("b1Wq", [512, 512]), ti("b1Wk", [512, 512]), \
        ti("b1Wv", [512, 512])
    b1Wo, b1bo = ti("b1Wo", [D, 512]), ti("b1bo", [D])
    b1wg, b1wu, b1wd = ti("b1wg", [4 * D, D]), ti("b1wu", [4 * D, D]), \
        ti("b1wd", [D, 4 * D])
    prm2 = {}
    for d, pfx in enumerate(("fwd", "bwd")):
        P = {}
        for nm, sh in (("log_real", [256]), ("imag", [256]),
                       ("log_Delta", [256]), ("D", [1024]),
                       ("B_real", [256, 1024]), ("B_imag", [256, 1024]),
                       ("C_real", [1024, 256]), ("C_imag", [1024, 256])):
            P[nm] = ti(pfx + "_" + nm, sh)
        prm2[d] = P
    b2wg, b2wfc, b2wpr = ti("b2wg", [8, D]), ti("w_fc", [8, 512, D]), \
        ti("w_proj", [8, D, 512])
    prm3 = {}
    for nm, sh in (("Wq", [D, D]), ("Wk", [D, D]), ("Wv", [D, D]),
                   ("lambda_params", [4]), ("g1", [D]), ("g2", [64]),
                   ("g3", [D]), ("W1", [D, D]), ("W2", [D, D]),
                   ("gate_w", [8, D]), ("gate_b", [8]),
                   ("w1", [8, D, 1408]), ("b1", [8, 1408]),
                   ("w3", [8, D, 1408]), ("b3", [8, 1408]),
                   ("w2", [8, 1408, D]), ("b2", [8, D]),
                   ("sw1", [D, 1408]), ("sb1", [1408]),
                   ("sw3", [D, 1408]), ("sb3", [1408]),
                   ("sw2", [1408, D]), ("sb2", [D])):
        prm3[nm] = ti("L3_" + nm, sh)
    y = nc.dram_tensor("y", [T, D], F32, kind="ExternalOutput").ap()
    itn = lambda nm, sh, dt: nc.dram_tensor(nm, sh, dt, kind="Internal").ap()
    x1, x2, x3 = itn("x1", [T, D], F32), itn("x2", [T, D], F32), \
        itn("x3", [T, D], F32)
    hd0, w2t0 = itn("hd0", [T, 4 * D], BF16), itn("w2t0", [4 * D, D], BF16)
    hd1, w2t1 = itn("hd1", [T, 4 * D], BF16), itn("w2t1", [4 * D, D], BF16)
    dram3 = {}
    for nm, sh, dt in (("QP2", [T, D], BF16), ("KP2", [T, D], BF16),
                       ("VP2", [T, D], BF16), ("AT2", [T, D], F32),
                       ("XND", [T, D], BF16)):
        dram3[nm] = itn("s_" + nm, sh, dt)
    env = std_env(nc, 47104)
    p = Prog(nc)
    AR, ST, SCR = env["AR"], env["ST"], env["SCR"]
    f32v, bf16v, i32v = arena_views(AR)
    st0 = emit_block0(p, env, x, b0W, b0w1, b0w2, x1, hd0, w2t0)
    emit_bitlinear_ffn(p, env, st0, [b0w1], b0w2, x1, hd0, w2t0, mode="gelu")
    p.set_barrier()
    st1 = emit_block1(p, env, x1, b1Wq, b1Wk, b1Wv, b1Wo, b1bo, x2)
    emit_bitlinear_ffn(p, env, st1, [b1wg, b1wu], b1wd, x2, hd1, w2t1,
                       mode="swiglu")
    p.set_barrier()
    c_all = emit_consts(p, env["IDB"], env["IDF"], env["TRI"], env["ONES32"],
                        env["ONESBF"])
    X = f32v(B0_X, NT * D)
    XN = f32v(B0_R2, NT * D)
    XNT = bf16v(B0_R3, 8192)
    Xs, xs2 = tok_major_pair(X, x2, NT)
    ld_x2 = p.dma(SP, Xs, xs2)
    ln2 = emit_ln(p, X, XN, ST, SC_LN, SCR, ld_x2)
    STG2 = [bf16v(B0_WK + 5120, 512), bf16v(B0_WK + 5632, 512)]
    xnt2 = emit_cast_transpose(
        p, env["banks"], [XN[:, i * D:(i + 1) * D] for i in range(NT)], XNT,
        STG2, env["IDB"], [[ln2[i], c_all[0]] for i in range(NT)])
    st2 = {"X": X, "XN": XN, "XNT": XNT, "consts": c_all, "ld_x": ld_x2,
           "ln_toks": ln2, "xnt_toks": xnt2, "dbg": None, "upto": "full"}
    s5_toks, scan_prev = emit_block2_s5(p, env, st2, prm2)
    emit_block2_moe(p, env, st2, b2wg, b2wfc, b2wpr, x3,
                    _flat([s5_toks, scan_prev]))
    p.set_barrier()
    c_all3 = emit_consts(p, env["IDB"], env["IDF"], env["TRI"], env["ONES32"],
                         env["ONESBF"])
    X3 = f32v(B0_X, NT * D)
    XN3 = f32v(B0_R2, NT * D)
    Xs3, xs3 = tok_major_pair(X3, x3, NT)
    ld_x3 = p.dma(SP, Xs3, xs3)
    ln3 = emit_ln(p, X3, XN3, ST, 128, SCR, ld_x3)
    st3 = {"X": X3, "XN": XN3, "consts": c_all3, "ld_x": ld_x3,
           "ln_toks": ln3, "dbg": None, "upto": "full"}
    mix = emit_block3_attn(p, env, st3, prm3, dram3)
    mlp = emit_block3_mlp(p, env, st3, prm3, dram3, mix)
    emit_block3_ffn(p, env, st3, prm3, dram3, mlp, y)
    p.emit(env["sems"])
    return nc


def kernel(**inputs):
    import numpy as _np
    from concourse.bass_utils import run_bass_kernel_spmd as _run

    ncores = 8

    def asf(a):
        return _np.ascontiguousarray(_np.asarray(a), dtype=_np.float32)

    x = asf(inputs["L0_attn_x"])
    common = {"W_qkv": asf(inputs["L0_attn_W_qkv"]),
              "b0w1": asf(inputs["L0_ffn_w1"]),
              "b0w2": asf(inputs["L0_ffn_w2"]),
              "b1Wq": asf(inputs["L1_attn_Wq"]),
              "b1Wk": asf(inputs["L1_attn_Wk"]),
              "b1Wv": asf(inputs["L1_attn_Wv"]),
              "b1Wo": asf(inputs["L1_attn_Wo"]),
              "b1bo": asf(inputs["L1_attn_bo"]),
              "b1wg": asf(inputs["L1_ffn_w_gate"]),
              "b1wu": asf(inputs["L1_ffn_w_up"]),
              "b1wd": asf(inputs["L1_ffn_w_down"]),
              "b2wg": asf(inputs["L2_ffn_w_gate"]),
              "w_fc": asf(inputs["L2_ffn_w_fc"]),
              "w_proj": asf(inputs["L2_ffn_w_proj"]),
              "fwd_log_real": asf(inputs["L2_attn_fwd_log_real"]),
              "fwd_imag": asf(inputs["L2_attn_fwd_imag"]),
              "fwd_log_Delta": asf(inputs["L2_attn_fwd_log_Delta"]),
              "fwd_D": asf(inputs["L2_attn_fwd_D"]),
              "fwd_B_real": asf(inputs["L2_attn_fwd_B_real"]),
              "fwd_B_imag": asf(inputs["L2_attn_fwd_B_imag"]),
              "fwd_C_real": asf(inputs["L2_attn_fwd_C_real"]),
              "fwd_C_imag": asf(inputs["L2_attn_fwd_C_imag"]),
              "bwd_log_real": asf(inputs["L2_attn_bwd_log_real"]),
              "bwd_imag": asf(inputs["L2_attn_bwd_imag"]),
              "bwd_log_Delta": asf(inputs["L2_attn_bwd_log_Delta"]),
              "bwd_D": asf(inputs["L2_attn_bwd_D"]),
              "bwd_B_real": asf(inputs["L2_attn_bwd_B_real"]),
              "bwd_B_imag": asf(inputs["L2_attn_bwd_B_imag"]),
              "bwd_C_real": asf(inputs["L2_attn_bwd_C_real"]),
              "bwd_C_imag": asf(inputs["L2_attn_bwd_C_imag"])}
    for nm in ("Wq", "Wk", "Wv", "lambda_params", "g1", "g2", "g3", "W1",
               "W2"):
        common["L3_" + nm] = asf(inputs["L3_attn_" + nm])
    for nm in ("gate_w", "gate_b", "w1", "b1", "w3", "b3", "w2", "b2",
               "sw1", "sb1", "sw3", "sb3", "sw2", "sb2"):
        common["L3_" + nm] = asf(inputs["L3_ffn_" + nm])
    nc = build_all()
    maps = [dict(common, x=x[c]) for c in range(ncores)]
    res = _run(nc, maps, core_ids=list(range(ncores)))
    return _np.stack([_np.asarray(res.results[c]["y"])
                      for c in range(ncores)])
